# version 10
# baseline (speedup 1.0000x reference)
import math
from contextlib import ExitStack

import numpy as np
import ml_dtypes

import concourse.bass as bass
import concourse.mybir as mybir
from concourse.bass_utils import run_bass_kernel_spmd

F32 = mybir.dt.float32
BF16 = mybir.dt.bfloat16
AF = mybir.ActivationFunctionType
ALU = mybir.AluOpType
AX = mybir.AxisListType

DM = 2048
NCH = 16
INTER = 8192
ICH = 64
TLOC = 2560
HALO = 512
TOWN = 2048
NH = 16
NKV = 4
DH = 128
EPS = 1e-5
RS2048 = 1.0 / math.sqrt(2048.0)
RS8192 = 1.0 / math.sqrt(8192.0)
NEG = -1e9
T = 512

ENGINES = ("sync", "gpsimd", "tensor", "scalar", "vector")


class Sem:
    def __init__(self, ctx, nc, name):
        self.h = ctx.enter_context(nc.semaphore(name))
        self.n = 0


class Sched:
    def __init__(self, nc):
        self.nc = nc
        self.q = {e: [] for e in ENGINES}

    def emit(self, eng, fn, waits=(), inc=None, inc_n=1):
        w = tuple((s.h, v) for (s, v) in waits if v > 0)
        if inc is not None:
            inc.n += inc_n
        self.q[eng].append((w, fn, None if inc is None else inc.h, inc_n))

    def run_block(self, blk):
        for eng in ENGINES:
            items = self.q[eng]
            if not items:
                continue

            def body(e, items=items):
                for waits, fn, inc_h, inc_n in items:
                    for (sem, val) in waits:
                        e.wait_ge(sem, val)
                    ins = fn(e) if fn is not None else None
                    if inc_h is not None:
                        if ins is None:
                            ins = e.nop()
                        ins.then_inc(inc_h, inc_n)

            getattr(blk, eng)(body)
        self.q = {e: [] for e in ENGINES}


class Prog:
    pass


def _chunked(ap2d, t):
    n = ap2d.shape[1] // t
    return ap2d.rearrange("p (n t) -> p n t", n=n, t=t)


def carve(raw, byte_off, p, n, dtype):
    c0 = byte_off // 2
    if dtype == BF16:
        return raw.ap()[0:p, c0:c0 + n]
    return raw.ap()[0:p, c0:c0 + 2 * n].bitcast(F32)


def build_program(passes=("prep", "ffn1", "attn", "ffn3"), xcur_out=False, dbg_scratch=False, detect_races=True, attn_kw=None):
    nc = bass.Bass("TRN2", target_bir_lowering=False,
                   detect_race_conditions=detect_races)
    P = Prog()
    P.nc = nc
    P.ctx = ExitStack()

    dt = nc.dram_tensor
    P.xin = dt("xin", [DM, TLOC], F32, kind="ExternalInput").ap()
    P.wupB = dt("wupB", [ICH, 128, NCH, 128], F32, kind="ExternalInput").ap()
    P.wgB = dt("wgB", [ICH, 128, NCH, 128], F32, kind="ExternalInput").ap()
    P.wdB = dt("wdB", [NCH, 128, ICH, 128], F32, kind="ExternalInput").ap()
    P.wqB = dt("wqB", [NH, 128, NCH, 128], F32, kind="ExternalInput").ap()
    P.wkB = dt("wkB", [NKV, 128, NCH, 128], F32, kind="ExternalInput").ap()
    P.wvB = dt("wvB", [NKV, 128, NCH, 128], F32, kind="ExternalInput").ap()
    P.woB = dt("woB", [NCH, 128, NCH, 128], F32, kind="ExternalInput").ap()
    P.su_pc = dt("su_pc", [128, NCH], F32, kind="ExternalInput").ap()
    P.wsu_pc = dt("wsu_pc", [128, ICH], F32, kind="ExternalInput").ap()
    P.wsg_pc = dt("wsg_pc", [128, ICH], F32, kind="ExternalInput").ap()
    P.sudp_pc = dt("sudp_pc", [128, ICH], F32, kind="ExternalInput").ap()
    P.wsdp_pc = dt("wsdp_pc", [128, NCH], F32, kind="ExternalInput").ap()
    P.h128 = dt("h128", [128, 128], BF16, kind="ExternalInput").ap()
    P.iden = dt("iden", [128, 128], BF16, kind="ExternalInput").ap()
    P.onesk = dt("onesk", [128, 1], BF16, kind="ExternalInput").ap()
    P.onesr = dt("onesr", [1, 128], F32, kind="ExternalInput").ap()
    P.onesb = dt("onesb", [1, 128], BF16, kind="ExternalInput").ap()
    P.cosq = dt("cosq", [128, TLOC], F32, kind="ExternalInput").ap()
    P.sinq = dt("sinq", [128, TLOC], F32, kind="ExternalInput").ap()
    P.cosk = dt("cosk", [128, TLOC], F32, kind="ExternalInput").ap()
    P.sink = dt("sink", [128, TLOC], F32, kind="ExternalInput").ap()
    P.krow = dt("krow", [1, TLOC], BF16, kind="ExternalInput").ap()
    P.bmsk = dt("bmsk", [128, 640], BF16, kind="ExternalInput").ap()
    P.fconst = dt("fconst", [1, 2], F32, kind="ExternalInput").ap()
    P.xcur = dt("xcur", [DM, TLOC], F32,
                kind="ExternalOutput" if xcur_out else "Internal").ap()
    sk = "ExternalOutput" if dbg_scratch else "Internal"
    P.wdck = dt("wdck", [NCH, 128, INTER], BF16, kind=sk).ap()
    P.wdbf = dt("wdbf", [NCH, 128, INTER], BF16, kind="Internal").ap()
    P.wugck = dt("wugck", [ICH, 128, 2 * DM], BF16, kind=sk).ap()
    P.wqck = dt("wqck", [NH, 128, DM], BF16, kind="Internal").ap()
    P.wkck = dt("wkck", [NKV, 128, DM], BF16, kind="Internal").ap()
    P.wvck = dt("wvck", [NKV, 128, DM], BF16, kind="Internal").ap()
    P.wock = dt("wock", [NCH, 128, DM], BF16, kind="Internal").ap()
    P.xout = dt("xout", [DM, TOWN], F32, kind="ExternalOutput").ap()

    E = P.ctx.enter_context
    P.hH = E(nc.sbuf_tensor("hH", [128, 128], BF16))
    P.idn = E(nc.sbuf_tensor("idn", [128, 128], BF16))
    P.onk = E(nc.sbuf_tensor("onk", [128, 1], BF16))
    P.onr = E(nc.sbuf_tensor("onr", [1, 128], F32))
    P.onb = E(nc.sbuf_tensor("onb", [1, 128], BF16))
    P.su_sb = E(nc.sbuf_tensor("su_sb", [128, NCH], F32))
    P.sg_sb = E(nc.sbuf_tensor("sg_sb", [128, ICH], F32))
    P.al_sb = E(nc.sbuf_tensor("al_sb", [128, ICH], F32))
    P.bdp_sb = E(nc.sbuf_tensor("bdp_sb", [128, NCH], F32))
    P.fc_sb = E(nc.sbuf_tensor("fc_sb", [1, 2], F32))
    class _PSem:
        pass
    P.c_ugc = []
    for i in range(8):
        ps = _PSem(); ps.h = E(nc.semaphore(f"c_ug{i}")); ps.n = 0
        P.c_ugc.append(ps)
    P.c_at = _PSem(); P.c_at.h = E(nc.semaphore("c_at")); P.c_at.n = 0

    if "prep" in passes:
        build_prep(P)
    if "ffn1" in passes:
        build_ffn_pass(P, src=P.xin, dst=P.xcur, col0=0, ncols=TLOC, tag="f1")
    if "attn" in passes:
        build_attn(P, **(attn_kw or {}))
    if "ffn3" in passes:
        build_ffn_pass(P, src=P.xcur, dst=P.xout, col0=HALO, ncols=TOWN,
                       tag="f3", dst_col_off=-HALO)
    P.ctx.close()
    return nc


def build_prep(P):
    nc = P.nc
    S = Sched(nc)
    with ExitStack() as ctx:
        E = ctx.enter_context
        ld = Sem(ctx, nc, "p_ld")
        wpl = [Sem(ctx, nc, f"p_wpl{i}") for i in range(2)]
        mm = Sem(ctx, nc, "p_mm")
        sc = Sem(ctx, nc, "p_sc")
        vc = Sem(ctx, nc, "p_vc")
        gp = Sem(ctx, nc, "p_gp")
        c_wd = Sem(ctx, nc, "p_cwd")
        c_at = P.c_at
        pso = [Sem(ctx, nc, f"p_pso{i}") for i in range(2)]

        wsu = E(nc.sbuf_tensor("p_wsu", [128, ICH], F32))
        sudp = E(nc.sbuf_tensor("p_sudp", [128, ICH], F32))
        pnl = [E(nc.sbuf_tensor(f"p_pnl{i}", [128, INTER], BF16)) for i in range(2)]
        wk32 = E(nc.sbuf_tensor("p_wk32", [128, INTER], F32))
        wk32b = E(nc.sbuf_tensor("p_wk32b", [128, INTER], F32))
        pwk32 = E(nc.sbuf_tensor("p_pwk32", [128, INTER], F32))
        pwk32b = E(nc.sbuf_tensor("p_pwk32b", [128, INTER], F32))
        pnlo = [E(nc.sbuf_tensor(f"p_pnlo{i}", [128, INTER], BF16)) for i in range(2)]
        psQ = [E(nc.psum_tensor(f"p_psQ{i}", [128, 512], F32)) for i in range(4)]

        S.emit("gpsimd", lambda e: e.dma_start(P.wdbf.flatten(),
                                               P.wdB.flatten()),
               inc=c_wd, inc_n=16)
        up3 = P.wupB.rearrange("c p k m -> c p (k m)")
        g3 = P.wgB.rearrange("c p k m -> c p (k m)")
        for ck in range(8):
            sl = slice(8 * ck, 8 * ck + 8)
            S.emit("gpsimd", lambda e, sl=sl: e.dma_start(
                P.wugck[sl, :, 0:DM], up3[sl]), inc=P.c_ugc[ck], inc_n=16)
            S.emit("gpsimd", lambda e, sl=sl: e.dma_start(
                P.wugck[sl, :, DM:2 * DM], g3[sl]), inc=P.c_ugc[ck], inc_n=16)
        for dstw, srcw in ((P.wqck, P.wqB), (P.wkck, P.wkB),
                           (P.wvck, P.wvB), (P.wock, P.woB)):
            S.emit("gpsimd", lambda e, dstw=dstw, srcw=srcw: e.dma_start(
                dstw.flatten(), srcw.flatten()), inc=c_at, inc_n=16)

        consts = [
            (P.hH[:, :], P.h128), (P.idn[:, :], P.iden),
            (P.onk[:, :], P.onesk), (P.onr[:, :], P.onesr), (P.onb[:, :], P.onesb),
            (P.su_sb[:, :], P.su_pc), (P.sg_sb[:, :], P.wsg_pc),
            (P.bdp_sb[:, :], P.wsdp_pc),
            (wsu[:, :], P.wsu_pc), (sudp[:, :], P.sudp_pc),
            (P.fc_sb[:, :], P.fconst),
        ]
        for d, s in consts:
            S.emit("sync", lambda e, d=d, s=s: e.dma_start(d, s), inc=ld, inc_n=16)
        n_const = ld.n

        S.emit("scalar",
               lambda e: e.activation(P.sg_sb[:, :], P.sg_sb[:, :], AF.Copy,
                                      scale=RS2048),
               waits=[(ld, n_const)])
        S.emit("vector",
               lambda e: e.scalar_tensor_tensor(P.al_sb[:, :], wsu[:, :], RS2048,
                                                sudp[:, :], ALU.mult, ALU.mult),
               waits=[(ld, n_const)], inc=vc)
        S.emit("scalar",
               lambda e: e.activation(P.bdp_sb[:, :], P.bdp_sb[:, :], AF.Copy,
                                      scale=0.5 * RS8192),
               waits=[(ld, n_const)])

        FR = 5440
        pe_done = {}
        s_evac = {}
        bf_v = {}
        for mp in range(NCH):
            par = mp % 2
            wkA, wkB_ = (wk32, wk32b) if par == 0 else (pwk32, pwk32b)
            w = [(mm, pe_done.get(mp - 2, 0)), (c_wd, 16)]
            S.emit("scalar",
                   lambda e, mp=mp, par=par: e.dma_start(
                       pnl[par][:, :], P.wdbf[mp]),
                   waits=w, inc=wpl[par], inc_n=16)
            for q in range(16):
                qb = q % 4
                w = [(wpl[par], wpl[par].n)]
                prev = s_evac.get((mp, q - 4), s_evac.get((mp - 1, 12 + q), 0))
                w.append((sc, prev))
                S.emit("tensor",
                       lambda e, q=q, qb=qb, par=par: e.matmul(
                           psQ[qb][:, :], P.hH[:, :],
                           pnl[par][:, 512 * q:512 * (q + 1)],
                           start=True, stop=True),
                       waits=w, inc=mm)
                ew = [(mm, mm.n)]
                if q == 0 and (mp - 2) in bf_v:
                    ew.append((vc, bf_v[mp - 2][0]))
                    ew.append((gp, bf_v[mp - 2][1]))
                S.emit("scalar",
                       lambda e, q=q, qb=qb, wkA=wkA: e.activation(
                           wkA[:, 512 * q:512 * (q + 1)], psQ[qb][:, :],
                           AF.Copy),
                       waits=ew, inc=sc)
                s_evac[(mp, q)] = sc.n
            pe_done[mp] = mm.n
            evac_done = sc.n

            src = wkA
            vstage = 0
            gstage = 0
            for st in range(6):
                dstb = pnlo[par] if st == 5 else (wkB_ if src is wkA else wkA)
                ns = ICH >> (st + 1)
                sbm = (1 << st) * 128
                svv = src[:, :].rearrange("p (ns two sbm) -> p ns two sbm",
                                          ns=ns, two=2, sbm=sbm)
                dvv = dstb[:, :].rearrange("p (ns two sbm) -> p ns two sbm",
                                           ns=ns, two=2, sbm=sbm)
                if sbm >= 2048:
                    cut = int(sbm * FR / 8192.0) // 2 * 2
                    dv = (svv[:, :, :, 0:cut], dvv[:, :, :, 0:cut])
                    gv = (svv[:, :, :, cut:], dvv[:, :, :, cut:])
                else:
                    nscut = max(min(int(round(ns * FR / 8192.0)), ns - 1), 1)
                    dv = (svv[:, 0:nscut], dvv[:, 0:nscut])
                    gv = (svv[:, nscut:], dvv[:, nscut:])
                wv = [(sc, evac_done)] if st == 0 else [(gp, gstage)]
                wg = [(sc, evac_done)] if st == 0 else [(vc, vstage)]
                if st == 0 and mp >= 2:
                    wv.append((pso[par], pso[par].n))
                    wg.append((pso[par], pso[par].n))
                S.emit("vector", lambda e, dv=dv: e.tensor_tensor(
                    dv[1][:, :, 0, :], dv[0][:, :, 0, :], dv[0][:, :, 1, :],
                    ALU.add), waits=wv)
                S.emit("vector", lambda e, dv=dv: e.tensor_tensor(
                    dv[1][:, :, 1, :], dv[0][:, :, 0, :], dv[0][:, :, 1, :],
                    ALU.subtract), inc=vc)
                vstage = vc.n
                S.emit("gpsimd", lambda e, gv=gv: e.tensor_tensor(
                    gv[1][:, :, 0, :], gv[0][:, :, 0, :], gv[0][:, :, 1, :],
                    ALU.add), waits=wg)
                S.emit("gpsimd", lambda e, gv=gv: e.tensor_tensor(
                    gv[1][:, :, 1, :], gv[0][:, :, 0, :], gv[0][:, :, 1, :],
                    ALU.subtract), inc=gp)
                gstage = gp.n
                src = dstb
            bf_v[mp] = (vstage, gstage)
            S.emit("sync",
                   lambda e, mp=mp, par=par: e.dma_start(P.wdck[mp],
                                                         pnlo[par][:, :]),
                   waits=[(vc, vstage), (gp, gstage)], inc=pso[par], inc_n=16)
        S.emit("sync", None, waits=[(pso[0], pso[0].n), (pso[1], pso[1].n),
                                    (vc, vc.n), (gp, gp.n), (sc, sc.n),
                                    (c_wd, 16)])

        with nc.Block("prep") as blk:
            S.run_block(blk)


def emit_ln(P, S, sems, bufs, src, col0, su, dest_chunk_fn,
            xld_waits=(), pe_waits=(), xbf_waits=()):
    ld, mm, sc, vc = sems["ld"], sems["mm"], sems["sc"], sems["vc"]
    xt, xbf = bufs["xt"], bufs["xbf"]
    mrow, vrow = bufs["mrow"], bufs["vrow"]
    psS, psM, psR = bufs["psS"], bufs["psM"], bufs["psR"]

    srcv = src[:, col0:col0 + T].rearrange("(kc p) t -> p kc t", p=128)
    S.emit("sync", lambda e: e.dma_start(_chunked(xt[:, 0:NCH * T], T), srcv),
           waits=xld_waits, inc=ld, inc_n=16)
    S.emit("scalar", lambda e: e.activation(xbf[:, 0:NCH * T], xt[:, 0:NCH * T],
                                            AF.Copy),
           waits=[(ld, ld.n)] + list(xbf_waits), inc=sc)

    def s1(e):
        ins = None
        for c in range(NCH):
            ins = e.matmul(psS[0:1, 0:T], P.onk[:, :],
                           xbf[:, c * T:(c + 1) * T],
                           start=(c == 0), stop=(c == NCH - 1))
        return ins
    S.emit("tensor", s1, waits=[(sc, sc.n)] + list(pe_waits), inc=mm)

    S.emit("scalar", lambda e: e.activation(mrow[0:1, 0:T], psS[0:1, 0:T],
                                            AF.Copy, scale=1.0 / DM),
           waits=[(mm, mm.n)], inc=sc)
    S.emit("tensor", lambda e: e.matmul(psM[:, 0:T], P.onr[0:1, :],
                                        mrow[0:1, 0:T], start=True, stop=True),
           waits=[(sc, sc.n)], inc=mm)

    def xm(e):
        ins = None
        for c in range(NCH):
            sl = slice(c * T, (c + 1) * T)
            ins = e.tensor_tensor(xt[:, sl], xt[:, sl], psM[:, 0:T],
                                  ALU.subtract)
        return ins
    S.emit("vector", xm, waits=[(mm, mm.n)], inc=vc)

    S.emit("scalar", lambda e: e.activation(xbf[:, 0:NCH * T], xt[:, 0:NCH * T],
                                            AF.Square),
           waits=[(vc, vc.n)], inc=sc)
    S.emit("tensor", s1, waits=[(sc, sc.n)], inc=mm)

    def vr(e):
        e.activation(vrow[0:1, 0:T], psS[0:1, 0:T], AF.Copy, scale=1.0 / DM)
        e.activation(vrow[0:1, 0:T], vrow[0:1, 0:T], AF.Ln, bias=P.fc_sb[0:1, 0:1])
        return e.activation(vrow[0:1, 0:T], vrow[0:1, 0:T], AF.Exp, scale=-0.5)
    S.emit("scalar", vr, waits=[(mm, mm.n)], inc=sc)

    S.emit("tensor", lambda e: e.matmul(psR[:, 0:T], P.onr[0:1, :],
                                        vrow[0:1, 0:T], start=True, stop=True),
           waits=[(sc, sc.n)], inc=mm)

    def stt(e):
        ins = None
        for c in range(NCH):
            scal = P.su_sb[:, c:c + 1] if su else 1.0
            ins = e.scalar_tensor_tensor(
                dest_chunk_fn(c), xt[:, c * T:(c + 1) * T], scal,
                psR[:, 0:T], ALU.mult, ALU.mult)
        return ins
    S.emit("vector", stt, waits=[(mm, mm.n)], inc=vc)
    return vc.n


def build_ffn_pass(P, src, dst, col0, ncols, tag, dst_col_off=0):
    nc = P.nc
    S = Sched(nc)
    ntiles = ncols // T
    TH = T // 2

    with ExitStack() as ctx:
        E = ctx.enter_context
        ld = Sem(ctx, nc, tag + "_ld")
        mm = Sem(ctx, nc, tag + "_mm")
        sc = Sem(ctx, nc, tag + "_sc")
        vc = Sem(ctx, nc, tag + "_vc")
        wld = [Sem(ctx, nc, tag + f"_wld{i}") for i in range(2)]
        dld = [Sem(ctx, nc, tag + f"_dld{i}") for i in range(2)]
        xml = [Sem(ctx, nc, tag + f"_xml{i}") for i in range(2)]
        stb = [Sem(ctx, nc, tag + f"_stb{i}") for i in range(2)]

        raw = E(nc.sbuf_tensor(tag + "_raw", [128, 20480], BF16))
        bfA = carve(raw, 0, 128, NCH * TH, F32)
        xbf = carve(raw, 0, 128, NCH * T, BF16)
        bfB = carve(raw, 16384, 128, NCH * TH, F32)
        y16 = carve(raw, 32768, 128, NCH * TH, BF16)

        hbuf = E(nc.sbuf_tensor(tag + "_h", [128, ICH * T], BF16))
        xt = E(nc.sbuf_tensor(tag + "_xt", [128, NCH * T], F32))
        yh = E(nc.sbuf_tensor(tag + "_yh", [128, NCH * T], BF16))
        mrow = E(nc.sbuf_tensor(tag + "_mrow", [1, T], F32))
        vrow = E(nc.sbuf_tensor(tag + "_vrow", [1, T], F32))
        gscr = [E(nc.sbuf_tensor(tag + f"_g{i}", [128, T], F32))
                for i in range(2)]
        pug = [E(nc.sbuf_tensor(tag + f"_pug{i}", [128, 2 * DM], BF16))
               for i in range(2)]
        dpnh = [E(nc.sbuf_tensor(tag + f"_dp{i}", [128, INTER // 2], BF16))
                for i in range(2)]
        xm = [E(nc.sbuf_tensor(tag + f"_xm{i}", [128, T], F32)) for i in range(2)]
        ob = [E(nc.sbuf_tensor(tag + f"_ob{i}", [128, T], F32)) for i in range(2)]
        psB = [E(nc.psum_tensor(tag + f"_ps{i}", [128, 512], F32))
               for i in range(8)]
        psS, psM, psR = psB[6], psB[7], psB[6]
        psH = [psB[0], psB[1]]

        stt_ln = {}
        bfly = {}
        h128e = {}
        stt_last = {}
        silu_cnt = {}
        fillA = {}
        grpB = {}
        pb_stt = {}
        xmsub = {}

        def ln_steps(t):
            tc0 = col0 + t * T
            steps = []
            srcv = src[:, tc0:tc0 + T].rearrange("(kc p) t -> p kc t", p=128)

            def s_load():
                w = [(vc, stt_ln.get(t - 1, 0))]
                S.emit("sync", lambda e: e.dma_start(
                    _chunked(xt[:, :], T), srcv), waits=w, inc=ld, inc_n=16)
                xl = ld.n
                w2 = [(ld, xl), (vc, bfly.get((t - 1, 1), 0))]
                if t >= 2:
                    w2.append((mm, grpB[(t - 2, NCH - 1, 1)]))
                S.emit("scalar", lambda e: e.activation(
                    xbf[:, :], xt[:, :], AF.Copy), waits=w2, inc=sc)
            steps.append(s_load)

            def mk_s1():
                def f(e):
                    ins = None
                    for c in range(NCH):
                        ins = e.matmul(psS[0:1, 0:T], P.onk[:, :],
                                       xbf[:, c * T:(c + 1) * T],
                                       start=(c == 0), stop=(c == NCH - 1))
                    return ins
                return f

            def s_s1():
                w = [(sc, sc.n), (vc, stt_ln.get(t - 1, 0))]
                S.emit("tensor", mk_s1(), waits=w, inc=mm)
                mr_mm = mm.n
                S.emit("scalar", lambda e: e.activation(
                    mrow[0:1, 0:T], psS[0:1, 0:T], AF.Copy, scale=1.0 / DM),
                    waits=[(mm, mr_mm)], inc=sc)
            steps.append(s_s1)

            def s_mb():
                w = [(sc, sc.n), (vc, xmsub.get(t - 1, 0))]
                S.emit("tensor", lambda e: e.matmul(
                    psM[:, 0:T], P.onr[0:1, :], mrow[0:1, 0:T],
                    start=True, stop=True), waits=w, inc=mm)
                mb_mm = mm.n

                def f(e):
                    ins = None
                    for c in range(NCH):
                        sl = slice(c * T, (c + 1) * T)
                        ins = e.tensor_tensor(xt[:, sl], xt[:, sl],
                                              psM[:, 0:T], ALU.subtract)
                    return ins
                S.emit("vector", f, waits=[(mm, mb_mm)], inc=vc)
                xmsub[t] = vc.n
                S.emit("scalar", lambda e: e.activation(
                    xbf[:, :], xt[:, :], AF.Square),
                    waits=[(vc, vc.n)], inc=sc)
            steps.append(s_mb)

            def s_s2():
                S.emit("tensor", mk_s1(), waits=[(sc, sc.n)], inc=mm)
                s2_mm = mm.n

                def f(e):
                    e.activation(vrow[0:1, 0:T], psS[0:1, 0:T], AF.Copy,
                                 scale=1.0 / DM)
                    e.activation(vrow[0:1, 0:T], vrow[0:1, 0:T], AF.Ln,
                                 bias=P.fc_sb[0:1, 0:1])
                    return e.activation(vrow[0:1, 0:T], vrow[0:1, 0:T],
                                        AF.Exp, scale=-0.5)
                S.emit("scalar", f, waits=[(mm, s2_mm)], inc=sc)
            steps.append(s_s2)

            def s_rb():
                S.emit("tensor", lambda e: e.matmul(
                    psR[:, 0:T], P.onr[0:1, :], vrow[0:1, 0:T],
                    start=True, stop=True), waits=[(sc, sc.n)], inc=mm)
            steps.append(s_rb)

            def mk_stt(half):
                def f():
                    def g(e):
                        ins = None
                        for c in range(NCH):
                            scal = P.su_sb[:, c:c + 1]
                            ins = e.scalar_tensor_tensor(
                                bfA[:, c * TH:(c + 1) * TH],
                                xt[:, c * T + half * TH:
                                   c * T + half * TH + TH],
                                scal, psR[:, half * TH:half * TH + TH],
                                ALU.mult, ALU.mult)
                        return ins
                    S.emit("vector", g, waits=[(mm, mm.n)], inc=vc)
                    if half == 1:
                        stt_ln[t] = vc.n
                return f

            def mk_bfly(half):
                def f():
                    srcb, dsts = bfA, [bfB, bfA, bfB, y16]
                    for s in range(4):
                        dstb = dsts[s]
                        ns = NCH >> (s + 1)
                        sbm = (1 << s) * TH
                        svv = srcb[:, 0:NCH * TH].rearrange(
                            "p (ns two sbm) -> p ns two sbm",
                            ns=ns, two=2, sbm=sbm)
                        dvv = dstb[:, 0:NCH * TH].rearrange(
                            "p (ns two sbm) -> p ns two sbm",
                            ns=ns, two=2, sbm=sbm)
                        S.emit("vector", lambda e, svv=svv, dvv=dvv:
                               e.tensor_tensor(dvv[:, :, 0, :], svv[:, :, 0, :],
                                               svv[:, :, 1, :], ALU.add))
                        S.emit("vector", lambda e, svv=svv, dvv=dvv:
                               e.tensor_tensor(dvv[:, :, 1, :], svv[:, :, 0, :],
                                               svv[:, :, 1, :], ALU.subtract),
                               inc=vc)
                        srcb = dstb
                    bfly[(t, half)] = vc.n
                return f

            def mk_h128(half, part):
                def f():
                    for c in range(8 * part, 8 * part + 8):
                        bank = c % 2
                        w = [(vc, bfly[(t, half)])]
                        idx = half * NCH + c
                        if (t, idx - 2) in h128e:
                            w.append((sc, h128e[(t, idx - 2)]))
                        elif t > 0 and idx < 2:
                            w.append((sc, h128e[(t - 1, 2 * NCH - 2 + idx)]))
                        if t > 0 and idx < 2:
                            w.append((vc, stt_last[(t - 1, ICH - 1)]))
                            w.append((sc, silu_cnt[(t - 1, ICH - 1)]))
                        S.emit("tensor",
                               lambda e, c=c, bank=bank: e.matmul(
                                   psH[bank][:, 0:TH], P.hH[:, :],
                                   y16[:, c * TH:(c + 1) * TH],
                                   start=True, stop=True),
                               waits=w, inc=mm)
                        S.emit("scalar",
                               lambda e, c=c, bank=bank, half=half:
                               e.activation(
                                   yh[:, c * T + half * TH:
                                      c * T + half * TH + TH],
                                   psH[bank][:, 0:TH], AF.Copy),
                               waits=[(mm, mm.n)], inc=sc)
                        h128e[(t, idx)] = sc.n
                return f

            steps.append(mk_stt(0))
            steps.append(mk_bfly(0))
            steps.append(mk_h128(0, 0))
            steps.append(mk_h128(0, 1))
            steps.append(mk_stt(1))
            steps.append(mk_bfly(1))
            steps.append(mk_h128(1, 0))
            steps.append(mk_h128(1, 1))
            return steps

        def phaseA(t, nxt_steps):
            inter = {0: 0, 6: 1, 10: 2, 14: 3, 18: 4, 40: 5}
            for ic in range(ICH):
                if ic in inter and nxt_steps:
                    nxt_steps[inter[ic]]()
                par = ic % 2
                w = []
                if (t, ic - 2) in fillA:
                    w = [(mm, fillA[(t, ic - 2)])]
                elif t > 0:
                    w = [(mm, fillA[(t - 1, ICH - 2 + ic)])]
                if t == 0 and src is P.xin:
                    w.append((P.c_ugc[ic // 8], 32))
                S.emit("sync", lambda e, ic=ic, par=par: e.dma_start(
                    pug[par][:, :], P.wugck[ic]),
                    waits=w, inc=wld[par], inc_n=16)

                uB, gB = psB[0 + 2 * par], psB[1 + 2 * par]
                w = [(wld[par], wld[par].n), (sc, h128e[(t, 2 * NCH - 1)])]
                if (t, ic - 2) in stt_last:
                    w.append((vc, stt_last[(t, ic - 2)]))
                else:
                    w.append((vc, bfly[(t, 1)]))

                def fillfn(e, par=par, uB=uB, gB=gB):
                    ins = None
                    for off, bk in ((0, uB), (DM, gB)):
                        for kc in range(NCH):
                            ins = e.matmul(bk[:, :],
                                           pug[par][:, off + kc * 128:
                                                    off + (kc + 1) * 128],
                                           yh[:, kc * T:(kc + 1) * T],
                                           start=(kc == 0),
                                           stop=(kc == NCH - 1))
                    return ins
                S.emit("tensor", fillfn, waits=w, inc=mm)
                fillA[(t, ic)] = mm.n

                gs = gscr[ic % 2]
                sw = [(mm, mm.n)]
                if (t, ic - 2) in stt_last:
                    sw.append((vc, stt_last[(t, ic - 2)]))
                S.emit("scalar", lambda e, ic=ic, gB=gB, gs=gs: e.activation(
                    gs[:, :], gB[:, :], AF.Silu,
                    scale=P.sg_sb[:, ic:ic + 1]),
                    waits=sw, inc=sc)
                silu_cnt[(t, ic)] = sc.n

                S.emit("vector", lambda e, ic=ic, uB=uB, gs=gs:
                       e.scalar_tensor_tensor(
                           hbuf[:, ic * T:(ic + 1) * T], uB[:, :],
                           P.al_sb[:, ic:ic + 1], gs[:, :], ALU.mult, ALU.mult),
                       waits=[(sc, sc.n)], inc=vc)
                stt_last[(t, ic)] = vc.n

        def phaseB(t, nxt_steps):
            tc0 = col0 + t * T
            inter = {1: 6, 2: 7, 3: 8, 5: 9, 7: 10, 9: 11, 11: 12}
            for m in range(NCH):
                if m in inter and nxt_steps:
                    nxt_steps[inter[m]]()
                par = m % 2
                dB = psB[4 + par]
                for hf in range(2):
                    w = []
                    if (t, m - 1, hf) in grpB:
                        w = [(mm, grpB[(t, m - 1, hf)])]
                    elif t > 0:
                        w = [(mm, grpB[(t - 1, NCH - 1, hf)])]
                    S.emit("scalar", lambda e, m=m, hf=hf: e.dma_start(
                        dpnh[hf][:, :],
                        P.wdck[m][:, hf * 4096:(hf + 1) * 4096]),
                        waits=w, inc=dld[hf], inc_n=16)
                w2 = []
                if (t, m - 2) in pb_stt:
                    w2 = [(vc, pb_stt[(t, m - 2)])]
                elif t > 0:
                    w2 = [(vc, pb_stt[(t - 1, NCH - 2 + m)])]
                S.emit("sync", lambda e, m=m, par=par, tc0=tc0: e.dma_start(
                    xm[par][:, :], src[m * 128:(m + 1) * 128, tc0:tc0 + T]),
                    waits=w2, inc=xml[par], inc_n=16)

                for hf in range(2):
                    w = [(dld[hf], dld[hf].n)]
                    if hf == 0:
                        if (t, m - 2) in pb_stt:
                            w.append((vc, pb_stt[(t, m - 2)]))
                        elif t > 0:
                            w.append((vc, pb_stt[(t - 1, NCH - 2 + m)]))
                        if m == 0:
                            w.append((vc, stt_last[(t, ICH - 1)]))

                    def dfill(e, par=par, hf=hf, dB=dB):
                        ins = None
                        for j in range(32):
                            kc = hf * 32 + j
                            ins = e.matmul(
                                dB[:, :], dpnh[hf][:, j * 128:(j + 1) * 128],
                                hbuf[:, kc * T:(kc + 1) * T],
                                start=(kc == 0), stop=(kc == ICH - 1))
                        return ins
                    S.emit("tensor", dfill, waits=w, inc=mm)
                    grpB[(t, m, hf)] = mm.n

                w = [(mm, mm.n), (xml[par], xml[par].n), (stb[par], stb[par].n)]
                S.emit("vector", lambda e, m=m, par=par, dB=dB:
                       e.scalar_tensor_tensor(
                           ob[par][:, :], dB[:, :], P.bdp_sb[:, m:m + 1],
                           xm[par][:, :], ALU.mult, ALU.add),
                       waits=w, inc=vc)
                pb_stt[(t, m)] = vc.n

                S.emit("sync", lambda e, m=m, par=par, tc0=tc0: e.dma_start(
                    dst[m * 128:(m + 1) * 128,
                        tc0 + dst_col_off: tc0 + dst_col_off + T],
                    ob[par][:, :]),
                    waits=[(vc, vc.n)], inc=stb[par], inc_n=16)

        st0 = ln_steps(0)
        for f in st0:
            f()
        for t in range(ntiles):
            nxt = ln_steps(t + 1) if t + 1 < ntiles else None
            phaseA(t, nxt)
            phaseB(t, nxt)

        S.emit("sync", None, waits=[(stb[0], stb[0].n), (stb[1], stb[1].n)])
        with nc.Block(tag) as blk:
            S.run_block(blk)

def build_attn(P, a1_only=False, serial_ln=False, no_maskmm=False,
               no_psumexp=False, no_p2=False, no_wo=False, s_only=False,
               no_p1c=False, s_extra=0):
    nc = P.nc
    S = Sched(nc)
    with ExitStack() as ctx:
        E = ctx.enter_context
        ld = Sem(ctx, nc, "a_ld")
        mm = Sem(ctx, nc, "a_mm")
        sc = Sem(ctx, nc, "a_sc")
        vc = Sem(ctx, nc, "a_vc")
        wld = [Sem(ctx, nc, f"a_wld{i}") for i in range(2)]
        xml = [Sem(ctx, nc, f"a_xml{i}") for i in range(2)]
        stb = [Sem(ctx, nc, f"a_stb{i}") for i in range(2)]
        csl = Sem(ctx, nc, "a_csl")

        raw = E(nc.sbuf_tensor("a_raw", [128, 33280], BF16))
        xt = carve(raw, 0, 128, NCH * T, F32)
        xbf = carve(raw, 32768, 128, NCH * T, BF16)
        cs = {}
        for i, nm in enumerate(("cq", "sq", "ck", "sk")):
            cs[nm] = carve(raw, 49152 + i * 2048, 128, T, F32)
        rsc = carve(raw, 57344, 128, T, F32)
        rsc2 = carve(raw, 59392, 128, T, F32)
        vtmp = carve(raw, 61440, 128, T, BF16)
        mrow = carve(raw, 62464, 1, T, F32)
        vrow = carve(raw, 64512, 1, T, F32)
        escr = [carve(raw, 0, 128, 640, F32), carve(raw, 2560, 128, 640, F32)]
        pb = [carve(raw, 5120, 128, 640, BF16), carve(raw, 6400, 128, 640, BF16)]
        pt = [carve(raw, 7680, 128, 640, BF16), carve(raw, 8960, 128, 640, BF16)]
        atile = carve(raw, 10240, 128, NH * T, BF16)
        xm = [carve(raw, 26624, 128, T, F32), carve(raw, 28672, 128, T, F32)]
        ob = [carve(raw, 30720, 128, T, F32), carve(raw, 32768, 128, T, F32)]

        ya = E(nc.sbuf_tensor("a_ya", [128, NCH * T], BF16))
        qbuf = E(nc.sbuf_tensor("a_q", [128, NH * TOWN], BF16))
        kbuf = E(nc.sbuf_tensor("a_k", [128, NKV * TLOC], BF16))
        vtb = E(nc.sbuf_tensor("a_vt", [128, NKV * TLOC], BF16))
        wpan = [E(nc.sbuf_tensor(f"a_wp{i}", [128, DM], BF16)) for i in range(2)]
        krw = E(nc.sbuf_tensor("a_krw", [1, TLOC], BF16))
        bmk = E(nc.sbuf_tensor("a_bmk", [128, 640], BF16))
        nmx = [E(nc.sbuf_tensor(f"a_nm{i}", [128, 1], F32)) for i in range(2)]
        ssm = [E(nc.sbuf_tensor(f"a_ss{i}", [128, 1], F32)) for i in range(2)]
        rcb = [E(nc.sbuf_tensor(f"a_rc{i}", [128, 1], F32)) for i in range(2)]

        psSab = [E(nc.psum_tensor(f"a_psS{i}", [128, 640], F32)) for i in range(2)]
        psO = [E(nc.psum_tensor(f"a_psO{i}", [128, 512], F32)) for i in range(2)]
        psT = E(nc.psum_tensor("a_psT", [128, 640], BF16))
        psX = E(nc.psum_tensor("a_psX", [128, 512], F32))
        psM = psSab[0][:, 0:512]

        S.emit("sync", lambda e: e.dma_start(krw[:, :], P.krow), inc=ld, inc_n=16)
        S.emit("sync", lambda e: e.dma_start(bmk[:, :], P.bmsk), inc=ld, inc_n=16)
        base_ld = ld.n

        stt_ln = {}
        xmsub = {}
        qkv_fill = {}
        slot_cons = {}
        rope_done = {}
        nocs = {}

        def ln_steps(t):
            tc0 = t * T
            srcv = P.xcur[:, tc0:tc0 + T].rearrange("(kc p) t -> p kc t", p=128)
            steps = []

            def s_load():
                w = [(vc, stt_ln.get(t - 1, 0))] if t > 0 else [(ld, base_ld)]
                S.emit("sync", lambda e: e.dma_start(
                    _chunked(xt[:, :], T), srcv), waits=w, inc=ld, inc_n=16)
                xl = ld.n
                S.emit("scalar", lambda e: e.activation(
                    xbf[:, :], xt[:, :], AF.Copy), waits=[(ld, xl)], inc=sc)
            steps.append(s_load)

            def mk_s1():
                def f(e):
                    ins = None
                    for c in range(NCH):
                        ins = e.matmul(psX[0:1, 0:T], P.onk[:, :],
                                       xbf[:, c * T:(c + 1) * T],
                                       start=(c == 0), stop=(c == NCH - 1))
                    return ins
                return f

            def s_s1():
                S.emit("tensor", mk_s1(),
                       waits=[(sc, sc.n), (vc, stt_ln.get(t - 1, 0))], inc=mm)
                S.emit("scalar", lambda e: e.activation(
                    mrow[0:1, 0:T], psX[0:1, 0:T], AF.Copy, scale=1.0 / DM),
                    waits=[(mm, mm.n)], inc=sc)
            steps.append(s_s1)

            def s_mb():
                w = [(sc, sc.n), (vc, xmsub.get(t - 1, 0))]
                S.emit("tensor", lambda e: e.matmul(
                    psM, P.onr[0:1, :], mrow[0:1, 0:T],
                    start=True, stop=True), waits=w, inc=mm)
                mb_mm = mm.n

                def f(e):
                    ins = None
                    for c in range(NCH):
                        sl = slice(c * T, (c + 1) * T)
                        ins = e.tensor_tensor(xt[:, sl], xt[:, sl], psM,
                                              ALU.subtract)
                    return ins
                S.emit("vector", f, waits=[(mm, mb_mm)], inc=vc)
                xmsub[t] = vc.n
                S.emit("scalar", lambda e: e.activation(
                    xbf[:, :], xt[:, :], AF.Square),
                    waits=[(vc, vc.n)], inc=sc)
            steps.append(s_mb)

            def s_s2():
                S.emit("tensor", mk_s1(), waits=[(sc, sc.n)], inc=mm)
                s2_mm = mm.n

                def f(e):
                    e.activation(vrow[0:1, 0:T], psX[0:1, 0:T], AF.Copy,
                                 scale=1.0 / DM)
                    e.activation(vrow[0:1, 0:T], vrow[0:1, 0:T], AF.Ln,
                                 bias=P.fc_sb[0:1, 0:1])
                    return e.activation(vrow[0:1, 0:T], vrow[0:1, 0:T],
                                        AF.Exp, scale=-0.5)
                S.emit("scalar", f, waits=[(mm, s2_mm)], inc=sc)
            steps.append(s_s2)

            def s_rb():
                S.emit("tensor", lambda e: e.matmul(
                    psX[:, 0:T], P.onr[0:1, :], vrow[0:1, 0:T],
                    start=True, stop=True), waits=[(sc, sc.n)], inc=mm)
            steps.append(s_rb)

            def s_stt():
                w = [(mm, mm.n)]
                if t > 0:
                    w.append((mm, qkv_fill[(t - 1, nocs[t - 1] - 1)]))

                def f(e):
                    ins = None
                    for c in range(NCH):
                        ins = e.scalar_tensor_tensor(
                            ya[:, c * T:(c + 1) * T], xt[:, c * T:(c + 1) * T],
                            1.0, psX[:, 0:T], ALU.mult, ALU.mult)
                    return ins
                S.emit("vector", f, waits=w, inc=vc)
                stt_ln[t] = vc.n
            steps.append(s_stt)

            def s_cs():
                for nm, srcap in (("cq", P.cosq), ("sq", P.sinq),
                                  ("ck", P.cosk), ("sk", P.sink)):
                    w = [(vc, rope_done.get(t - 1, 0))]
                    S.emit("sync", lambda e, nm=nm, srcap=srcap: e.dma_start(
                        cs[nm][:, :], srcap[:, tc0:tc0 + T]), waits=w,
                        inc=csl, inc_n=16)
            steps.append(s_cs)
            return steps

        for t in range(5):
            tc0 = t * T
            has_q = tc0 + T > HALO
            ocs = []
            if has_q:
                for h in range(NH):
                    ocs.append(("q", h, P.wqck))
            for h in range(NKV):
                ocs.append(("k", h, P.wkck))
            for h in range(NKV):
                ocs.append(("v", h, P.wvck))
            nocs[t] = len(ocs)

            if t == 0:
                for f in ln_steps(0):
                    f()
            if serial_ln and t > 0:
                for f in ln_steps(t):
                    f()
            nxt = None if serial_ln else (ln_steps(t + 1) if t < 4 else None)
            n_oc = len(ocs)
            inter = {}
            if nxt:
                step_pos = [1, 3, 5, 7, 9] if n_oc >= 11 else [1, 2, 3, 4, 5]
                for j, p_ in enumerate(step_pos):
                    inter[p_] = j

            for oi, (kind, h, wsrc) in enumerate(ocs):
                if nxt and oi in inter:
                    nxt[inter[oi]]()
                par = oi % 2
                bank = psO[par]
                w = []
                if (t, oi - 2) in qkv_fill:
                    w = [(mm, qkv_fill[(t, oi - 2)])]
                elif t > 0:
                    w = [(mm, qkv_fill[(t - 1, nocs[t - 1] - 2 + oi)])]
                else:
                    w = [(P.c_at, P.c_at.n)]
                S.emit("scalar", lambda e, h=h, wsrc=wsrc, par=par: e.dma_start(
                    wpan[par][:, :], wsrc[h]),
                    waits=w, inc=wld[par], inc_n=16)

                w = [(wld[par], wld[par].n), (vc, stt_ln[t])]
                pc = slot_cons.get((t, oi - 2))
                if pc is None and t > 0:
                    pc = slot_cons.get((t - 1, nocs[t - 1] - 2 + oi))
                if pc is not None:
                    w.append((pc[0], pc[1]))

                def qf(e, par=par, bank=bank):
                    ins = None
                    for kc in range(NCH):
                        ins = e.matmul(bank[:, :],
                                       wpan[par][:, kc * 128:(kc + 1) * 128],
                                       ya[:, kc * T:(kc + 1) * T],
                                       start=(kc == 0), stop=(kc == NCH - 1))
                    return ins
                S.emit("tensor", qf, waits=w, inc=mm)
                qkv_fill[(t, oi)] = mm.n

                if kind in ("q", "k"):
                    co, so = (cs["cq"], cs["sq"]) if kind == "q" else                              (cs["ck"], cs["sk"])
                    if kind == "q":
                        dest = qbuf[:, h * TOWN + (tc0 - HALO):
                                    h * TOWN + (tc0 - HALO) + T]
                    else:
                        dest = kbuf[:, h * TLOC + tc0: h * TLOC + tc0 + T]

                    def rope(e, bank=bank, co=co, so=so, dest=dest):
                        e.tensor_tensor(rsc[:, :], bank[:, :], co[:, :],
                                        ALU.mult)
                        e.tensor_tensor(rsc2[0:64, :], bank[64:128, :],
                                        so[0:64, :], ALU.mult)
                        e.tensor_tensor(rsc2[64:128, :], bank[0:64, :],
                                        so[64:128, :], ALU.mult)
                        return e.tensor_tensor(dest, rsc[:, :], rsc2[:, :],
                                               ALU.add)
                    S.emit("vector", rope, waits=[(mm, mm.n), (csl, csl.n)],
                           inc=vc)
                    rope_done[t] = vc.n
                    slot_cons[(t, oi)] = (vc, vc.n)
                else:
                    S.emit("scalar", lambda e, bank=bank: e.activation(
                        vtmp[:, :], bank[:, :], AF.Copy),
                        waits=[(mm, mm.n)], inc=sc)
                    slot_cons[(t, oi)] = (sc, sc.n)
                    for b in range(4):
                        S.emit("tensor", lambda e, b=b: e.transpose(
                            psT[:, 0:128], vtmp[:, b * 128:(b + 1) * 128],
                            P.idn[:, :]),
                            waits=[(sc, sc.n)], inc=mm)
                        blk_i = tc0 // 128 + b
                        S.emit("scalar",
                               lambda e, h=h, blk_i=blk_i: e.activation(
                                   vtb[:, (h * 20 + blk_i) * 128:
                                       (h * 20 + blk_i) * 128 + 128],
                                   psT[:, 0:128], AF.Copy),
                               waits=[(mm, mm.n)], inc=sc)
            if nxt:
                nxt[5]()
                nxt[6]()
        a1_vc = vc.n
        a1_sc = sc.n
        a1_mm = mm.n

        if a1_only:
            S.emit("sync", None, waits=[(vc, a1_vc), (sc, a1_sc), (mm, a1_mm),
                                        (csl, csl.n)])
            with nc.Block("attn") as blk:
                S.run_block(blk)
            return

        NIT = 4 * NH
        sfill = {}
        nmx_v = {}
        exp_s = {}
        sum_v = {}
        psc_v = {}
        tr_m = {}
        pte_s = {}
        pv_m = {}
        oev_s = {}
        wo_fill = {}
        wo_stt = {}

        for wt in range(4):
            t0own = wt * T
            iters = [(t0own // 128 + qb4, h) for qb4 in range(4)
                     for h in range(NH)]

            def kprev(d, it, wt=wt):
                if it >= 0:
                    return d.get((wt, it))
                if wt > 0:
                    return d.get((wt - 1, NIT + it))
                return None

            def emit_S(it, wt=wt, iters=iters):
                i, h = iters[it]
                par = it % 2
                sAB = psSab[par]
                kv = h // NKV
                qsl = qbuf[:, h * TOWN + i * 128: h * TOWN + i * 128 + 128]
                w = []
                if s_only:
                    pv = kprev(nmx_v, it - 2)
                    if pv is not None:
                        w.append((vc, pv))
                    else:
                        w.append((vc, a1_vc))
                        w.append((sc, a1_sc))
                else:
                    pv = kprev(exp_s, it - 2)
                    if pv is not None:
                        w.append((sc, pv))
                    else:
                        w.append((vc, a1_vc))
                        w.append((sc, a1_sc))
                if no_p1c:
                    pv = kprev(sum_v, it - 2)
                    if pv is not None:
                        w.append((vc, pv))

                def f(e, i=i, kv=kv, sAB=sAB, qsl=qsl):
                    e.matmul(sAB[:, 0:512], P.onb[0:1, :],
                             krw[0:1, i * 128: i * 128 + 512],
                             start=True, stop=False)
                    if not no_maskmm:
                        e.matmul(sAB[:, 0:512], P.idn[:, :], bmk[:, 0:512],
                                 start=False, stop=False)
                    e.matmul(sAB[:, 0:512], qsl,
                             kbuf[:, kv * TLOC + i * 128:
                                  kv * TLOC + i * 128 + 512],
                             start=False, stop=True)
                    if not no_maskmm:
                        e.matmul(sAB[:, 512:640], P.idn[:, :], bmk[:, 512:640],
                                 start=True, stop=False)
                        return e.matmul(sAB[:, 512:640], qsl,
                                        kbuf[:, kv * TLOC + i * 128 + 512:
                                             kv * TLOC + i * 128 + 640],
                                        start=False, stop=True)
                    return e.matmul(sAB[:, 512:640], qsl,
                                    kbuf[:, kv * TLOC + i * 128 + 512:
                                         kv * TLOC + i * 128 + 640],
                                    start=True, stop=True)
                S.emit("tensor", f, waits=w, inc=mm)
                sfill[(wt, it)] = mm.n

            def p1a(it, wt=wt):
                par = it % 2
                w = [(mm, sfill[(wt, it)])]
                pv = kprev(exp_s, it - 2)
                if pv is not None:
                    w.append((sc, pv))
                if s_only:
                    S.emit("vector", lambda e, par=par: e.tensor_copy(
                        escr[par][:, :], psSab[par][:, :]), waits=w, inc=vc)
                    if s_extra == 1:
                        S.emit("vector", lambda e, par=par: e.tensor_reduce(
                            nmx[par][:, 0:1], psSab[par][:, :], AX.X, ALU.max,
                            negate=True), inc=vc)
                    elif s_extra == 2:
                        S.emit("vector", lambda e, par=par: e.tensor_reduce(
                            nmx[par][:, 0:1], escr[par][:, :], AX.X, ALU.max,
                            negate=True), inc=vc)
                    elif s_extra == 3:
                        S.emit("scalar", lambda e, par=par: e.activation(
                            pb[par][:, :], escr[par][:, :], AF.Exp), inc=sc)
                    elif s_extra == 6:
                        S.emit("vector", lambda e, par=par: e.tensor_reduce(
                            nmx[par][:, 0:1], escr[par][:, :], AX.X, ALU.max,
                            negate=True), inc=vc)
                        S.emit("scalar", lambda e, par=par: e.activation(
                            pb[par][:, :], escr[par][:, :], AF.Exp,
                            bias=nmx[par][:, 0:1]),
                            waits=[(vc, vc.n)], inc=sc)
                    elif s_extra == 7:
                        S.emit("vector", lambda e, par=par: e.tensor_reduce(
                            nmx[par][:, 0:1], escr[par][:, :], AX.X, ALU.max,
                            negate=True), inc=vc)
                        S.emit("scalar", lambda e, par=par: e.activation(
                            pb[par][:, :], escr[par][:, :], AF.Exp,
                            bias=nmx[par][:, 0:1]),
                            waits=[(vc, vc.n)], inc=sc)
                        S.emit("vector", lambda e, par=par: e.tensor_reduce(
                            ssm[par][:, 0:1], pb[par][:, :], AX.X, ALU.add),
                            waits=[(sc, sc.n)], inc=vc)
                    elif s_extra == 4:
                        S.emit("vector", lambda e, par=par: e.tensor_reduce(
                            nmx[par][:, 0:1], psSab[par][:, :], AX.X, ALU.max,
                            negate=True), inc=vc)
                        S.emit("scalar", lambda e, par=par: e.activation(
                            escr[par][:, :], psSab[par][:, :], AF.Exp,
                            bias=nmx[par][:, 0:1]),
                            waits=[(vc, vc.n)], inc=sc)
                    nmx_v[(wt, it)] = vc.n
                    return
                if no_psumexp:
                    pv2 = kprev(psc_v, it - 2)
                    if pv2 is not None:
                        w.append((vc, pv2))
                    S.emit("vector", lambda e, par=par: e.tensor_copy(
                        escr[par][:, :], psSab[par][:, :]), waits=w, inc=vc)
                    S.emit("vector", lambda e, par=par: e.tensor_reduce(
                        nmx[par][:, 0:1], escr[par][:, :], AX.X, ALU.max,
                        negate=True), inc=vc)
                else:
                    S.emit("vector", lambda e, par=par: e.tensor_reduce(
                        nmx[par][:, 0:1], psSab[par][:, :], AX.X, ALU.max,
                        negate=True),
                        waits=w, inc=vc)
                nmx_v[(wt, it)] = vc.n

            def p1b(it, wt=wt):
                par = it % 2
                w = [(vc, nmx_v[(wt, it)])]
                pv = kprev(psc_v, it - 2)
                if pv is not None:
                    w.append((vc, pv))
                if no_psumexp:
                    S.emit("scalar", lambda e, par=par: e.activation(
                        escr[par][:, :], escr[par][:, :], AF.Exp,
                        bias=nmx[par][:, 0:1]),
                        waits=w, inc=sc)
                else:
                    S.emit("scalar", lambda e, par=par: e.activation(
                        escr[par][:, :], psSab[par][:, :], AF.Exp,
                        bias=nmx[par][:, 0:1]),
                        waits=w, inc=sc)
                exp_s[(wt, it)] = sc.n
                S.emit("vector", lambda e, par=par: e.tensor_reduce(
                    ssm[par][:, 0:1], escr[par][:, :], AX.X, ALU.add),
                    waits=[(sc, sc.n)], inc=vc)
                sum_v[(wt, it)] = vc.n

            def p1c(it, wt=wt):
                par = it % 2

                def rcp(e, par=par):
                    e.activation(rcb[par][:, 0:1], ssm[par][:, 0:1], AF.Ln)
                    return e.activation(rcb[par][:, 0:1], rcb[par][:, 0:1],
                                        AF.Exp, scale=-1.0)
                S.emit("scalar", rcp, waits=[(vc, sum_v[(wt, it)])], inc=sc)
                w = [(sc, sc.n)]
                pv = kprev(tr_m, it - 2)
                if pv is not None:
                    w.append((mm, pv))
                S.emit("vector", lambda e, par=par: e.tensor_scalar_mul(
                    pb[par][:, :], escr[par][:, :], rcb[par][:, 0:1]),
                    waits=w, inc=vc)
                psc_v[(wt, it)] = vc.n

            def p2(it, wt=wt, iters=iters):
                i, h = iters[it]
                par = it % 2
                kv = h // NKV
                qb4 = i - (wt * 4)
                w = [(vc, psc_v[(wt, it)])]
                pv = kprev(pte_s, it - 1)
                if pv is not None:
                    w.append((sc, pv))

                def tr(e, par=par):
                    ins = None
                    for j in range(5):
                        ins = e.transpose(psT[:, j * 128:(j + 1) * 128],
                                          pb[par][:, j * 128:(j + 1) * 128],
                                          P.idn[:, :])
                    return ins
                S.emit("tensor", tr, waits=w, inc=mm)
                tr_m[(wt, it)] = mm.n
                w = [(mm, mm.n)]
                pv = kprev(pv_m, it - 2)
                if pv is not None:
                    w.append((mm, pv))
                S.emit("scalar", lambda e, par=par: e.activation(
                    pt[par][:, :], psT[:, :], AF.Copy),
                    waits=w, inc=sc)
                pte_s[(wt, it)] = sc.n

                w = [(sc, sc.n)]
                if it >= 2:
                    w.append((sc, oev_s[(wt, it - 2)]))
                elif wt > 0:
                    w.append((vc, wo_stt[(wt - 1, NCH - 2 + it)]))

                def pvf(e, i=i, kv=kv, par=par):
                    ins = None
                    for j in range(5):
                        blk_i = i + j
                        ins = e.matmul(psO[par][:, 0:128],
                                       vtb[:, (kv * 20 + blk_i) * 128:
                                           (kv * 20 + blk_i) * 128 + 128],
                                       pt[par][:, j * 128:(j + 1) * 128],
                                       start=(j == 0), stop=(j == 4))
                    return ins
                S.emit("tensor", pvf, waits=w, inc=mm)
                pv_m[(wt, it)] = mm.n
                S.emit("scalar", lambda e, h=h, qb4=qb4, par=par: e.activation(
                    atile[:, h * T + qb4 * 128: h * T + qb4 * 128 + 128],
                    psO[par][:, 0:128], AF.Copy),
                    waits=[(mm, mm.n)], inc=sc)
                oev_s[(wt, it)] = sc.n

            for i in range(NIT + 4):
                if 1 <= i <= NIT:
                    p1a(i - 1)
                if 2 <= i <= NIT + 1 and not s_only:
                    p1b(i - 2)
                if 3 <= i <= NIT + 2 and not s_only and not no_p1c:
                    p1c(i - 3)
                if 4 <= i and not no_p2:
                    p2(i - 4)
                if i < NIT:
                    emit_S(i)
            if no_p2 or no_wo:
                if wt == 3:
                    S.emit("sync", None,
                           waits=[(vc, vc.n), (sc, sc.n), (mm, mm.n)])
                    with nc.Block("attn") as blk:
                        S.run_block(blk)
                    return
                continue
            a_done_sc = oev_s[(wt, NIT - 1)]

            for m in range(NCH):
                par = m % 2
                w = []
                if (wt, m - 2) in wo_fill:
                    w = [(mm, wo_fill[(wt, m - 2)])]
                elif wt > 0:
                    w = [(mm, wo_fill[(wt - 1, NCH - 2 + m)])]
                else:
                    w = [(mm, a1_mm)]
                S.emit("scalar", lambda e, m=m, par=par: e.dma_start(
                    wpan[par][:, :], P.wock[m]),
                    waits=w, inc=wld[par], inc_n=16)
                w2 = []
                if (wt, m - 2) in wo_stt:
                    w2 = [(vc, wo_stt[(wt, m - 2)])]
                elif wt > 0:
                    w2 = [(vc, wo_stt[(wt - 1, NCH - 2 + m)])]
                else:
                    w2 = [(mm, a1_mm), (vc, a1_vc), (sc, a1_sc)]
                S.emit("sync", lambda e, m=m, par=par, t0own=t0own: e.dma_start(
                    xm[par][:, :],
                    P.xcur[m * 128:(m + 1) * 128,
                           HALO + t0own: HALO + t0own + T]),
                    waits=w2, inc=xml[par], inc_n=16)

                w = [(wld[par], wld[par].n), (sc, a_done_sc)]
                if (wt, m - 2) in wo_stt:
                    w.append((vc, wo_stt[(wt, m - 2)]))
                elif wt > 0 and m < 2:
                    w.append((vc, wo_stt[(wt - 1, NCH - 2 + m)]))

                def wof(e, par=par):
                    ins = None
                    for kc in range(NCH):
                        ins = e.matmul(psO[par][:, :],
                                       wpan[par][:, kc * 128:(kc + 1) * 128],
                                       atile[:, kc * T:(kc + 1) * T],
                                       start=(kc == 0), stop=(kc == NCH - 1))
                    return ins
                S.emit("tensor", wof, waits=w, inc=mm)
                wo_fill[(wt, m)] = mm.n
                w = [(mm, mm.n), (xml[par], xml[par].n), (stb[par], stb[par].n)]
                S.emit("vector", lambda e, par=par: e.tensor_tensor(
                    ob[par][:, :], psO[par][:, :], xm[par][:, :], ALU.add),
                    waits=w, inc=vc)
                wo_stt[(wt, m)] = vc.n
                S.emit("sync", lambda e, m=m, par=par, t0own=t0own: e.dma_start(
                    P.xcur[m * 128:(m + 1) * 128,
                           HALO + t0own: HALO + t0own + T],
                    ob[par][:, :]),
                    waits=[(vc, vc.n)], inc=stb[par], inc_n=16)

        S.emit("sync", None, waits=[(stb[0], stb[0].n), (stb[1], stb[1].n)])
        with nc.Block("attn") as blk:
            S.run_block(blk)


def _host_inputs(L0_attn_x, L0_attn_Wq, L0_attn_Wk, L0_attn_Wv, L0_attn_Wo,
                 L0_ffn_SU_ug, L0_ffn_SU_dp, L0_ffn_Wscale_ug, L0_ffn_Wscale_dp,
                 L0_ffn_W_up, L0_ffn_W_gate, L0_ffn_W_down):
    bf = ml_dtypes.bfloat16
    f32 = np.float32

    def blockw(W, npan):
        W = np.asarray(W, f32)
        o, i = W.shape
        return np.ascontiguousarray(
            W.reshape(npan, 128, i // 128, 128).transpose(0, 3, 2, 1))

    shared = {
        "wupB": blockw(L0_ffn_W_up, ICH),
        "wgB": blockw(L0_ffn_W_gate, ICH),
        "wdB": blockw(L0_ffn_W_down, NCH),
        "wqB": blockw(L0_attn_Wq, NH),
        "wkB": blockw(L0_attn_Wk, NKV),
        "wvB": blockw(L0_attn_Wv, NKV),
        "woB": blockw(L0_attn_Wo, NCH),
        "su_pc": np.ascontiguousarray(
            np.asarray(L0_ffn_SU_ug, f32).reshape(NCH, 128).T),
        "wsu_pc": np.ascontiguousarray(
            np.asarray(L0_ffn_Wscale_ug, f32)[:INTER].reshape(ICH, 128).T),
        "wsg_pc": np.ascontiguousarray(
            np.asarray(L0_ffn_Wscale_ug, f32)[INTER:].reshape(ICH, 128).T),
        "sudp_pc": np.ascontiguousarray(
            np.asarray(L0_ffn_SU_dp, f32).reshape(ICH, 128).T),
        "wsdp_pc": np.ascontiguousarray(
            np.asarray(L0_ffn_Wscale_dp, f32).reshape(NCH, 128).T),
    }
    ij = np.arange(128)
    pop = np.zeros((128, 128), np.int64)
    for b in range(7):
        pop += ((ij[:, None] >> b) & 1) * ((ij[None, :] >> b) & 1)
    shared["h128"] = np.where(pop % 2 == 0, 1.0, -1.0).astype(bf)
    shared["iden"] = np.eye(128).astype(bf)
    shared["onesk"] = np.ones((128, 1), bf)
    shared["onesr"] = np.ones((1, 128), f32)
    shared["onesb"] = np.ones((1, 128), bf)
    p = np.arange(128)[:, None]
    x = np.arange(640)[None, :]
    valid = np.where(x < 512, x > p, (x - 512) <= p)
    shared["bmsk"] = np.where(valid, 0.0, NEG).astype(bf)
    shared["fconst"] = np.array([[EPS, 0.0]], f32)

    x4 = np.asarray(L0_attn_x, f32)
    scale = 1.0 / math.sqrt(DH)
    inv_freq = (10000.0 ** (-np.arange(0, DH, 2, dtype=np.float64) / DH))

    in_maps = []
    for core in range(8):
        b, half = divmod(core, 2)
        if half == 0:
            xloc = np.concatenate(
                [np.zeros((HALO, DM), f32), x4[b, 0:TOWN]], axis=0)
            abspos = np.arange(-HALO, TOWN)
            krow = np.concatenate(
                [np.full(HALO, NEG, f32), np.zeros(TOWN, f32)])
        else:
            xloc = x4[b, TOWN - HALO: 4096]
            abspos = np.arange(TOWN - HALO, 4096)
            krow = np.zeros(TLOC, f32)
        pos = np.maximum(abspos, 0)
        ang = pos[:, None].astype(np.float64) * inv_freq[None, :]
        c64 = np.cos(ang).astype(f32)
        s64 = np.sin(ang).astype(f32)
        cosf = np.concatenate([c64, c64], axis=1).T
        sinf = np.concatenate([-s64, s64], axis=1).T
        m = dict(shared)
        m["xin"] = np.ascontiguousarray(xloc.T)
        m["cosq"] = np.ascontiguousarray(cosf * scale).astype(f32)
        m["sinq"] = np.ascontiguousarray(sinf * scale).astype(f32)
        m["cosk"] = np.ascontiguousarray(cosf)
        m["sink"] = np.ascontiguousarray(sinf)
        m["krow"] = krow[None, :].astype(bf)
        in_maps.append(m)
    return in_maps


def kernel(**inputs):
    in_maps = _host_inputs(**inputs)
    nc = build_program()
    res = run_bass_kernel_spmd(nc, in_maps, core_ids=list(range(8)))
    out = np.empty((4, 4096, DM), np.float32)
    for core in range(8):
        b, half = divmod(core, 2)
        out[b, half * TOWN:(half + 1) * TOWN, :] = res.results[core]["xout"].T
    return out


# version 12
# speedup vs baseline: 1.0174x; 1.0174x over previous
import math
from contextlib import ExitStack

import numpy as np
import ml_dtypes

import concourse.bass as bass
import concourse.mybir as mybir
from concourse.bass_utils import run_bass_kernel_spmd

F32 = mybir.dt.float32
BF16 = mybir.dt.bfloat16
AF = mybir.ActivationFunctionType
ALU = mybir.AluOpType
AX = mybir.AxisListType

DM = 2048
NCH = 16
INTER = 8192
ICH = 64
TLOC = 2560
HALO = 512
TOWN = 2048
NH = 16
NKV = 4
DH = 128
EPS = 1e-5
RS2048 = 1.0 / math.sqrt(2048.0)
RS8192 = 1.0 / math.sqrt(8192.0)
NEG = -1e9
T = 512

ENGINES = ("sync", "gpsimd", "tensor", "scalar", "vector")


class Sem:
    def __init__(self, ctx, nc, name):
        self.h = ctx.enter_context(nc.semaphore(name))
        self.n = 0


class Sched:
    def __init__(self, nc):
        self.nc = nc
        self.q = {e: [] for e in ENGINES}

    def emit(self, eng, fn, waits=(), inc=None, inc_n=1):
        w = tuple((s.h, v) for (s, v) in waits if v > 0)
        if inc is not None:
            inc.n += inc_n
        self.q[eng].append((w, fn, None if inc is None else inc.h, inc_n))

    def run_block(self, blk):
        for eng in ENGINES:
            items = self.q[eng]
            if not items:
                continue

            def body(e, items=items):
                for waits, fn, inc_h, inc_n in items:
                    for (sem, val) in waits:
                        e.wait_ge(sem, val)
                    ins = fn(e) if fn is not None else None
                    if inc_h is not None:
                        if ins is None:
                            ins = e.nop()
                        ins.then_inc(inc_h, inc_n)

            getattr(blk, eng)(body)
        self.q = {e: [] for e in ENGINES}


class Prog:
    pass


def _chunked(ap2d, t):
    n = ap2d.shape[1] // t
    return ap2d.rearrange("p (n t) -> p n t", n=n, t=t)


def carve(raw, byte_off, p, n, dtype):
    c0 = byte_off // 2
    if dtype == BF16:
        return raw.ap()[0:p, c0:c0 + n]
    return raw.ap()[0:p, c0:c0 + 2 * n].bitcast(F32)


def build_program(passes=("prep", "ffn1", "attn", "ffn3"), xcur_out=False, dbg_scratch=False, detect_races=True, attn_kw=None):
    nc = bass.Bass("TRN2", target_bir_lowering=False,
                   detect_race_conditions=detect_races)
    P = Prog()
    P.nc = nc
    P.ctx = ExitStack()

    dt = nc.dram_tensor
    P.xin = dt("xin", [DM, TLOC], F32, kind="ExternalInput").ap()
    P.wupB = dt("wupB", [ICH, 128, NCH, 128], F32, kind="ExternalInput").ap()
    P.wgB = dt("wgB", [ICH, 128, NCH, 128], F32, kind="ExternalInput").ap()
    P.wdB = dt("wdB", [NCH, 128, ICH, 128], F32, kind="ExternalInput").ap()
    P.wqB = dt("wqB", [NH, 128, NCH, 128], F32, kind="ExternalInput").ap()
    P.wkB = dt("wkB", [NKV, 128, NCH, 128], F32, kind="ExternalInput").ap()
    P.wvB = dt("wvB", [NKV, 128, NCH, 128], F32, kind="ExternalInput").ap()
    P.woB = dt("woB", [NCH, 128, NCH, 128], F32, kind="ExternalInput").ap()
    P.su_pc = dt("su_pc", [128, NCH], F32, kind="ExternalInput").ap()
    P.wsu_pc = dt("wsu_pc", [128, ICH], F32, kind="ExternalInput").ap()
    P.wsg_pc = dt("wsg_pc", [128, ICH], F32, kind="ExternalInput").ap()
    P.sudp_pc = dt("sudp_pc", [128, ICH], F32, kind="ExternalInput").ap()
    P.wsdp_pc = dt("wsdp_pc", [128, NCH], F32, kind="ExternalInput").ap()
    P.h128 = dt("h128", [128, 128], BF16, kind="ExternalInput").ap()
    P.iden = dt("iden", [128, 128], BF16, kind="ExternalInput").ap()
    P.onesk = dt("onesk", [128, 1], BF16, kind="ExternalInput").ap()
    P.onesr = dt("onesr", [1, 128], F32, kind="ExternalInput").ap()
    P.onesb = dt("onesb", [1, 128], BF16, kind="ExternalInput").ap()
    P.cosq = dt("cosq", [128, TLOC], F32, kind="ExternalInput").ap()
    P.sinq = dt("sinq", [128, TLOC], F32, kind="ExternalInput").ap()
    P.cosk = dt("cosk", [128, TLOC], F32, kind="ExternalInput").ap()
    P.sink = dt("sink", [128, TLOC], F32, kind="ExternalInput").ap()
    P.krow = dt("krow", [1, TLOC], BF16, kind="ExternalInput").ap()
    P.bmsk = dt("bmsk", [128, 640], BF16, kind="ExternalInput").ap()
    P.fconst = dt("fconst", [1, 2], F32, kind="ExternalInput").ap()
    P.xcur = dt("xcur", [DM, TLOC], F32,
                kind="ExternalOutput" if xcur_out else "Internal").ap()
    sk = "ExternalOutput" if dbg_scratch else "Internal"
    P.wdck = dt("wdck", [NCH, 128, INTER], BF16, kind=sk).ap()
    P.wdbf = dt("wdbf", [NCH, 128, INTER], BF16, kind="Internal").ap()
    P.wugck = dt("wugck", [ICH, 128, 2 * DM], BF16, kind=sk).ap()
    P.wqck = dt("wqck", [NH, 128, DM], BF16, kind="Internal").ap()
    P.wkck = dt("wkck", [NKV, 128, DM], BF16, kind="Internal").ap()
    P.wvck = dt("wvck", [NKV, 128, DM], BF16, kind="Internal").ap()
    P.wock = dt("wock", [NCH, 128, DM], BF16, kind="Internal").ap()
    P.xout = dt("xout", [DM, TOWN], F32, kind="ExternalOutput").ap()

    E = P.ctx.enter_context
    P.hH = E(nc.sbuf_tensor("hH", [128, 128], BF16))
    P.idn = E(nc.sbuf_tensor("idn", [128, 128], BF16))
    P.onk = E(nc.sbuf_tensor("onk", [128, 1], BF16))
    P.onr = E(nc.sbuf_tensor("onr", [1, 128], F32))
    P.onb = E(nc.sbuf_tensor("onb", [1, 128], BF16))
    P.su_sb = E(nc.sbuf_tensor("su_sb", [128, NCH], F32))
    P.sg_sb = E(nc.sbuf_tensor("sg_sb", [128, ICH], F32))
    P.al_sb = E(nc.sbuf_tensor("al_sb", [128, ICH], F32))
    P.bdp_sb = E(nc.sbuf_tensor("bdp_sb", [128, NCH], F32))
    P.fc_sb = E(nc.sbuf_tensor("fc_sb", [1, 2], F32))
    class _PSem:
        pass
    P.c_ugc = []
    for i in range(8):
        ps = _PSem(); ps.h = E(nc.semaphore(f"c_ug{i}")); ps.n = 0
        P.c_ugc.append(ps)
    P.c_at = _PSem(); P.c_at.h = E(nc.semaphore("c_at")); P.c_at.n = 0

    if "prep" in passes:
        build_prep(P)
    if "ffn1" in passes:
        build_ffn_pass(P, src=P.xin, dst=P.xcur, col0=0, ncols=TLOC, tag="f1")
    if "attn" in passes:
        build_attn(P, **(attn_kw or {}))
    if "ffn3" in passes:
        build_ffn_pass(P, src=P.xcur, dst=P.xout, col0=HALO, ncols=TOWN,
                       tag="f3", dst_col_off=-HALO)
    P.ctx.close()
    return nc


def build_prep(P):
    nc = P.nc
    S = Sched(nc)
    with ExitStack() as ctx:
        E = ctx.enter_context
        ld = Sem(ctx, nc, "p_ld")
        wpl = [Sem(ctx, nc, f"p_wpl{i}") for i in range(2)]
        mm = Sem(ctx, nc, "p_mm")
        sc = Sem(ctx, nc, "p_sc")
        vc = Sem(ctx, nc, "p_vc")
        gp = Sem(ctx, nc, "p_gp")
        c_at = P.c_at
        pso = [Sem(ctx, nc, f"p_pso{i}") for i in range(2)]

        wsu = E(nc.sbuf_tensor("p_wsu", [128, ICH], F32))
        sudp = E(nc.sbuf_tensor("p_sudp", [128, ICH], F32))
        pnl = [E(nc.sbuf_tensor(f"p_pnl{i}", [128, INTER], BF16)) for i in range(2)]
        wk32 = E(nc.sbuf_tensor("p_wk32", [128, INTER], F32))
        wk32b = E(nc.sbuf_tensor("p_wk32b", [128, INTER], F32))
        pwk32 = E(nc.sbuf_tensor("p_pwk32", [128, INTER], F32))
        pwk32b = E(nc.sbuf_tensor("p_pwk32b", [128, INTER], F32))
        pnlo = [E(nc.sbuf_tensor(f"p_pnlo{i}", [128, INTER], BF16)) for i in range(2)]
        psQ = [E(nc.psum_tensor(f"p_psQ{i}", [128, 512], F32)) for i in range(4)]

        wdc = [Sem(ctx, nc, f"p_wdc{i}") for i in range(NCH)]
        for mp_ in range(NCH):
            S.emit("gpsimd", lambda e, mp_=mp_: e.dma_start(
                P.wdbf[mp_], P.wdB[mp_].rearrange("p c q -> p (c q)")),
                inc=wdc[mp_], inc_n=16)
        up3 = P.wupB.rearrange("c p k m -> c p (k m)")
        g3 = P.wgB.rearrange("c p k m -> c p (k m)")
        for ck in range(8):
            sl = slice(8 * ck, 8 * ck + 8)
            S.emit("gpsimd", lambda e, sl=sl: e.dma_start(
                P.wugck[sl, :, 0:DM], up3[sl]), inc=P.c_ugc[ck], inc_n=16)
            S.emit("gpsimd", lambda e, sl=sl: e.dma_start(
                P.wugck[sl, :, DM:2 * DM], g3[sl]), inc=P.c_ugc[ck], inc_n=16)
        for dstw, srcw in ((P.wqck, P.wqB), (P.wkck, P.wkB),
                           (P.wvck, P.wvB), (P.wock, P.woB)):
            S.emit("gpsimd", lambda e, dstw=dstw, srcw=srcw: e.dma_start(
                dstw.flatten(), srcw.flatten()), inc=c_at, inc_n=16)

        consts = [
            (P.hH[:, :], P.h128), (P.idn[:, :], P.iden),
            (P.onk[:, :], P.onesk), (P.onr[:, :], P.onesr), (P.onb[:, :], P.onesb),
            (P.su_sb[:, :], P.su_pc), (P.sg_sb[:, :], P.wsg_pc),
            (P.bdp_sb[:, :], P.wsdp_pc),
            (wsu[:, :], P.wsu_pc), (sudp[:, :], P.sudp_pc),
            (P.fc_sb[:, :], P.fconst),
        ]
        for d, s in consts:
            S.emit("sync", lambda e, d=d, s=s: e.dma_start(d, s), inc=ld, inc_n=16)
        n_const = ld.n

        S.emit("scalar",
               lambda e: e.activation(P.sg_sb[:, :], P.sg_sb[:, :], AF.Copy,
                                      scale=RS2048),
               waits=[(ld, n_const)])
        S.emit("vector",
               lambda e: e.scalar_tensor_tensor(P.al_sb[:, :], wsu[:, :], RS2048,
                                                sudp[:, :], ALU.mult, ALU.mult),
               waits=[(ld, n_const)], inc=vc)
        S.emit("scalar",
               lambda e: e.activation(P.bdp_sb[:, :], P.bdp_sb[:, :], AF.Copy,
                                      scale=0.5 * RS8192),
               waits=[(ld, n_const)])

        FR = 5440
        pe_done = {}
        s_evac = {}
        bf_v = {}
        for mp in range(NCH):
            par = mp % 2
            wkA, wkB_ = (wk32, wk32b) if par == 0 else (pwk32, pwk32b)
            w = [(mm, pe_done.get(mp - 2, 0)), (wdc[mp], 16)]
            S.emit("scalar",
                   lambda e, mp=mp, par=par: e.dma_start(
                       pnl[par][:, :], P.wdbf[mp]),
                   waits=w, inc=wpl[par], inc_n=16)
            for q in range(16):
                qb = q % 4
                w = [(wpl[par], wpl[par].n)]
                prev = s_evac.get((mp, q - 4), s_evac.get((mp - 1, 12 + q), 0))
                w.append((sc, prev))
                S.emit("tensor",
                       lambda e, q=q, qb=qb, par=par: e.matmul(
                           psQ[qb][:, :], P.hH[:, :],
                           pnl[par][:, 512 * q:512 * (q + 1)],
                           start=True, stop=True),
                       waits=w, inc=mm)
                ew = [(mm, mm.n)]
                if q == 0 and (mp - 2) in bf_v:
                    ew.append((vc, bf_v[mp - 2][0]))
                    ew.append((gp, bf_v[mp - 2][1]))
                S.emit("scalar",
                       lambda e, q=q, qb=qb, wkA=wkA: e.activation(
                           wkA[:, 512 * q:512 * (q + 1)], psQ[qb][:, :],
                           AF.Copy),
                       waits=ew, inc=sc)
                s_evac[(mp, q)] = sc.n
            pe_done[mp] = mm.n
            evac_done = sc.n

            src = wkA
            vstage = 0
            gstage = 0
            for st in range(6):
                dstb = pnlo[par] if st == 5 else (wkB_ if src is wkA else wkA)
                ns = ICH >> (st + 1)
                sbm = (1 << st) * 128
                svv = src[:, :].rearrange("p (ns two sbm) -> p ns two sbm",
                                          ns=ns, two=2, sbm=sbm)
                dvv = dstb[:, :].rearrange("p (ns two sbm) -> p ns two sbm",
                                           ns=ns, two=2, sbm=sbm)
                if sbm >= 2048:
                    cut = int(sbm * FR / 8192.0) // 2 * 2
                    dv = (svv[:, :, :, 0:cut], dvv[:, :, :, 0:cut])
                    gv = (svv[:, :, :, cut:], dvv[:, :, :, cut:])
                else:
                    nscut = max(min(int(round(ns * FR / 8192.0)), ns - 1), 1)
                    dv = (svv[:, 0:nscut], dvv[:, 0:nscut])
                    gv = (svv[:, nscut:], dvv[:, nscut:])
                wv = [(sc, evac_done)] if st == 0 else [(gp, gstage)]
                wg = [(sc, evac_done)] if st == 0 else [(vc, vstage)]
                if st == 0 and mp >= 2:
                    wv.append((pso[par], pso[par].n))
                    wg.append((pso[par], pso[par].n))
                S.emit("vector", lambda e, dv=dv: e.tensor_tensor(
                    dv[1][:, :, 0, :], dv[0][:, :, 0, :], dv[0][:, :, 1, :],
                    ALU.add), waits=wv)
                S.emit("vector", lambda e, dv=dv: e.tensor_tensor(
                    dv[1][:, :, 1, :], dv[0][:, :, 0, :], dv[0][:, :, 1, :],
                    ALU.subtract), inc=vc)
                vstage = vc.n
                S.emit("gpsimd", lambda e, gv=gv: e.tensor_tensor(
                    gv[1][:, :, 0, :], gv[0][:, :, 0, :], gv[0][:, :, 1, :],
                    ALU.add), waits=wg)
                S.emit("gpsimd", lambda e, gv=gv: e.tensor_tensor(
                    gv[1][:, :, 1, :], gv[0][:, :, 0, :], gv[0][:, :, 1, :],
                    ALU.subtract), inc=gp)
                gstage = gp.n
                src = dstb
            bf_v[mp] = (vstage, gstage)
            S.emit("sync",
                   lambda e, mp=mp, par=par: e.dma_start(P.wdck[mp],
                                                         pnlo[par][:, :]),
                   waits=[(vc, vstage), (gp, gstage)], inc=pso[par], inc_n=16)
        S.emit("sync", None, waits=[(pso[0], pso[0].n), (pso[1], pso[1].n),
                                    (vc, vc.n), (gp, gp.n), (sc, sc.n)])

        with nc.Block("prep") as blk:
            S.run_block(blk)


def emit_ln(P, S, sems, bufs, src, col0, su, dest_chunk_fn,
            xld_waits=(), pe_waits=(), xbf_waits=()):
    ld, mm, sc, vc = sems["ld"], sems["mm"], sems["sc"], sems["vc"]
    xt, xbf = bufs["xt"], bufs["xbf"]
    mrow, vrow = bufs["mrow"], bufs["vrow"]
    psS, psM, psR = bufs["psS"], bufs["psM"], bufs["psR"]

    srcv = src[:, col0:col0 + T].rearrange("(kc p) t -> p kc t", p=128)
    S.emit("sync", lambda e: e.dma_start(_chunked(xt[:, 0:NCH * T], T), srcv),
           waits=xld_waits, inc=ld, inc_n=16)
    S.emit("scalar", lambda e: e.activation(xbf[:, 0:NCH * T], xt[:, 0:NCH * T],
                                            AF.Copy),
           waits=[(ld, ld.n)] + list(xbf_waits), inc=sc)

    def s1(e):
        ins = None
        for c in range(NCH):
            ins = e.matmul(psS[0:1, 0:T], P.onk[:, :],
                           xbf[:, c * T:(c + 1) * T],
                           start=(c == 0), stop=(c == NCH - 1))
        return ins
    S.emit("tensor", s1, waits=[(sc, sc.n)] + list(pe_waits), inc=mm)

    S.emit("scalar", lambda e: e.activation(mrow[0:1, 0:T], psS[0:1, 0:T],
                                            AF.Copy, scale=1.0 / DM),
           waits=[(mm, mm.n)], inc=sc)
    S.emit("tensor", lambda e: e.matmul(psM[:, 0:T], P.onr[0:1, :],
                                        mrow[0:1, 0:T], start=True, stop=True),
           waits=[(sc, sc.n)], inc=mm)

    def xm(e):
        ins = None
        for c in range(NCH):
            sl = slice(c * T, (c + 1) * T)
            ins = e.tensor_tensor(xt[:, sl], xt[:, sl], psM[:, 0:T],
                                  ALU.subtract)
        return ins
    S.emit("vector", xm, waits=[(mm, mm.n)], inc=vc)

    S.emit("scalar", lambda e: e.activation(xbf[:, 0:NCH * T], xt[:, 0:NCH * T],
                                            AF.Square),
           waits=[(vc, vc.n)], inc=sc)
    S.emit("tensor", s1, waits=[(sc, sc.n)], inc=mm)

    def vr(e):
        e.activation(vrow[0:1, 0:T], psS[0:1, 0:T], AF.Copy, scale=1.0 / DM)
        e.activation(vrow[0:1, 0:T], vrow[0:1, 0:T], AF.Ln, bias=P.fc_sb[0:1, 0:1])
        return e.activation(vrow[0:1, 0:T], vrow[0:1, 0:T], AF.Exp, scale=-0.5)
    S.emit("scalar", vr, waits=[(mm, mm.n)], inc=sc)

    S.emit("tensor", lambda e: e.matmul(psR[:, 0:T], P.onr[0:1, :],
                                        vrow[0:1, 0:T], start=True, stop=True),
           waits=[(sc, sc.n)], inc=mm)

    def stt(e):
        ins = None
        for c in range(NCH):
            scal = P.su_sb[:, c:c + 1] if su else 1.0
            ins = e.scalar_tensor_tensor(
                dest_chunk_fn(c), xt[:, c * T:(c + 1) * T], scal,
                psR[:, 0:T], ALU.mult, ALU.mult)
        return ins
    S.emit("vector", stt, waits=[(mm, mm.n)], inc=vc)
    return vc.n


def build_ffn_pass(P, src, dst, col0, ncols, tag, dst_col_off=0):
    nc = P.nc
    S = Sched(nc)
    ntiles = ncols // T
    TH = T // 2

    with ExitStack() as ctx:
        E = ctx.enter_context
        ld = Sem(ctx, nc, tag + "_ld")
        mm = Sem(ctx, nc, tag + "_mm")
        sc = Sem(ctx, nc, tag + "_sc")
        vc = Sem(ctx, nc, tag + "_vc")
        wld = [Sem(ctx, nc, tag + f"_wld{i}") for i in range(2)]
        dld = [Sem(ctx, nc, tag + f"_dld{i}") for i in range(2)]
        xml = [Sem(ctx, nc, tag + f"_xml{i}") for i in range(2)]
        stb = [Sem(ctx, nc, tag + f"_stb{i}") for i in range(2)]

        raw = E(nc.sbuf_tensor(tag + "_raw", [128, 20480], BF16))
        bfA = carve(raw, 0, 128, NCH * TH, F32)
        xbf = carve(raw, 0, 128, NCH * T, BF16)
        bfB = carve(raw, 16384, 128, NCH * TH, F32)
        y16 = carve(raw, 32768, 128, NCH * TH, BF16)

        hbuf = E(nc.sbuf_tensor(tag + "_h", [128, ICH * T], BF16))
        xt = E(nc.sbuf_tensor(tag + "_xt", [128, NCH * T], F32))
        yh = E(nc.sbuf_tensor(tag + "_yh", [128, NCH * T], BF16))
        mrow = E(nc.sbuf_tensor(tag + "_mrow", [1, T], F32))
        vrow = E(nc.sbuf_tensor(tag + "_vrow", [1, T], F32))
        gscr = [E(nc.sbuf_tensor(tag + f"_g{i}", [128, T], F32))
                for i in range(2)]
        pug = [E(nc.sbuf_tensor(tag + f"_pug{i}", [128, 2 * DM], BF16))
               for i in range(2)]
        dpnh = [E(nc.sbuf_tensor(tag + f"_dp{i}", [128, INTER // 2], BF16))
                for i in range(2)]
        xm = [E(nc.sbuf_tensor(tag + f"_xm{i}", [128, T], F32)) for i in range(2)]
        ob = [E(nc.sbuf_tensor(tag + f"_ob{i}", [128, T], F32)) for i in range(2)]
        psB = [E(nc.psum_tensor(tag + f"_ps{i}", [128, 512], F32))
               for i in range(8)]
        psS, psM, psR = psB[6], psB[7], psB[6]
        psH = [psB[0], psB[1]]

        stt_ln = {}
        bfly = {}
        h128e = {}
        stt_last = {}
        silu_cnt = {}
        fillA = {}
        grpB = {}
        pb_stt = {}
        xmsub = {}

        def ln_steps(t):
            tc0 = col0 + t * T
            steps = []
            srcv = src[:, tc0:tc0 + T].rearrange("(kc p) t -> p kc t", p=128)

            def s_load():
                w = [(vc, stt_ln.get(t - 1, 0))]
                S.emit("sync", lambda e: e.dma_start(
                    _chunked(xt[:, :], T), srcv), waits=w, inc=ld, inc_n=16)
                xl = ld.n
                w2 = [(ld, xl), (vc, bfly.get((t - 1, 1), 0))]
                if t >= 2:
                    w2.append((mm, grpB[(t - 2, NCH - 1, 1)]))
                S.emit("scalar", lambda e: e.activation(
                    xbf[:, :], xt[:, :], AF.Copy), waits=w2, inc=sc)
            steps.append(s_load)

            def mk_s1():
                def f(e):
                    ins = None
                    for c in range(NCH):
                        ins = e.matmul(psS[0:1, 0:T], P.onk[:, :],
                                       xbf[:, c * T:(c + 1) * T],
                                       start=(c == 0), stop=(c == NCH - 1))
                    return ins
                return f

            def s_s1():
                w = [(sc, sc.n), (vc, stt_ln.get(t - 1, 0))]
                S.emit("tensor", mk_s1(), waits=w, inc=mm)
                mr_mm = mm.n
                S.emit("scalar", lambda e: e.activation(
                    mrow[0:1, 0:T], psS[0:1, 0:T], AF.Copy, scale=1.0 / DM),
                    waits=[(mm, mr_mm)], inc=sc)
            steps.append(s_s1)

            def s_mb():
                w = [(sc, sc.n), (vc, xmsub.get(t - 1, 0))]
                S.emit("tensor", lambda e: e.matmul(
                    psM[:, 0:T], P.onr[0:1, :], mrow[0:1, 0:T],
                    start=True, stop=True), waits=w, inc=mm)
                mb_mm = mm.n

                def f(e):
                    ins = None
                    for c in range(NCH):
                        sl = slice(c * T, (c + 1) * T)
                        ins = e.tensor_tensor(xt[:, sl], xt[:, sl],
                                              psM[:, 0:T], ALU.subtract)
                    return ins
                S.emit("vector", f, waits=[(mm, mb_mm)], inc=vc)
                xmsub[t] = vc.n
                S.emit("scalar", lambda e: e.activation(
                    xbf[:, :], xt[:, :], AF.Square),
                    waits=[(vc, vc.n)], inc=sc)
            steps.append(s_mb)

            def s_s2():
                S.emit("tensor", mk_s1(), waits=[(sc, sc.n)], inc=mm)
                s2_mm = mm.n

                def f(e):
                    e.activation(vrow[0:1, 0:T], psS[0:1, 0:T], AF.Copy,
                                 scale=1.0 / DM)
                    e.activation(vrow[0:1, 0:T], vrow[0:1, 0:T], AF.Ln,
                                 bias=P.fc_sb[0:1, 0:1])
                    return e.activation(vrow[0:1, 0:T], vrow[0:1, 0:T],
                                        AF.Exp, scale=-0.5)
                S.emit("scalar", f, waits=[(mm, s2_mm)], inc=sc)
            steps.append(s_s2)

            def s_rb():
                S.emit("tensor", lambda e: e.matmul(
                    psR[:, 0:T], P.onr[0:1, :], vrow[0:1, 0:T],
                    start=True, stop=True), waits=[(sc, sc.n)], inc=mm)
            steps.append(s_rb)

            def mk_stt(half):
                def f():
                    def g(e):
                        ins = None
                        for c in range(NCH):
                            scal = P.su_sb[:, c:c + 1]
                            ins = e.scalar_tensor_tensor(
                                bfA[:, c * TH:(c + 1) * TH],
                                xt[:, c * T + half * TH:
                                   c * T + half * TH + TH],
                                scal, psR[:, half * TH:half * TH + TH],
                                ALU.mult, ALU.mult)
                        return ins
                    S.emit("vector", g, waits=[(mm, mm.n)], inc=vc)
                    if half == 1:
                        stt_ln[t] = vc.n
                return f

            def mk_bfly(half):
                def f():
                    srcb, dsts = bfA, [bfB, bfA, bfB, y16]
                    for s in range(4):
                        dstb = dsts[s]
                        ns = NCH >> (s + 1)
                        sbm = (1 << s) * TH
                        svv = srcb[:, 0:NCH * TH].rearrange(
                            "p (ns two sbm) -> p ns two sbm",
                            ns=ns, two=2, sbm=sbm)
                        dvv = dstb[:, 0:NCH * TH].rearrange(
                            "p (ns two sbm) -> p ns two sbm",
                            ns=ns, two=2, sbm=sbm)
                        S.emit("vector", lambda e, svv=svv, dvv=dvv:
                               e.tensor_tensor(dvv[:, :, 0, :], svv[:, :, 0, :],
                                               svv[:, :, 1, :], ALU.add))
                        S.emit("vector", lambda e, svv=svv, dvv=dvv:
                               e.tensor_tensor(dvv[:, :, 1, :], svv[:, :, 0, :],
                                               svv[:, :, 1, :], ALU.subtract),
                               inc=vc)
                        srcb = dstb
                    bfly[(t, half)] = vc.n
                return f

            def mk_h128(half, part):
                def f():
                    for c in range(8 * part, 8 * part + 8):
                        bank = c % 2
                        w = [(vc, bfly[(t, half)])]
                        idx = half * NCH + c
                        if (t, idx - 2) in h128e:
                            w.append((sc, h128e[(t, idx - 2)]))
                        elif t > 0 and idx < 2:
                            w.append((sc, h128e[(t - 1, 2 * NCH - 2 + idx)]))
                        if t > 0 and idx < 2:
                            w.append((vc, stt_last[(t - 1, ICH - 1)]))
                            w.append((sc, silu_cnt[(t - 1, ICH - 1)]))
                        S.emit("tensor",
                               lambda e, c=c, bank=bank: e.matmul(
                                   psH[bank][:, 0:TH], P.hH[:, :],
                                   y16[:, c * TH:(c + 1) * TH],
                                   start=True, stop=True),
                               waits=w, inc=mm)
                        S.emit("scalar",
                               lambda e, c=c, bank=bank, half=half:
                               e.activation(
                                   yh[:, c * T + half * TH:
                                      c * T + half * TH + TH],
                                   psH[bank][:, 0:TH], AF.Copy),
                               waits=[(mm, mm.n)], inc=sc)
                        h128e[(t, idx)] = sc.n
                return f

            steps.append(mk_stt(0))
            steps.append(mk_bfly(0))
            steps.append(mk_h128(0, 0))
            steps.append(mk_h128(0, 1))
            steps.append(mk_stt(1))
            steps.append(mk_bfly(1))
            steps.append(mk_h128(1, 0))
            steps.append(mk_h128(1, 1))
            return steps

        def phaseA(t, nxt_steps):
            inter = {0: 0, 6: 1, 10: 2, 14: 3, 18: 4, 40: 5}
            for ic in range(ICH):
                if ic in inter and nxt_steps:
                    nxt_steps[inter[ic]]()
                par = ic % 2
                w = []
                if (t, ic - 2) in fillA:
                    w = [(mm, fillA[(t, ic - 2)])]
                elif t > 0:
                    w = [(mm, fillA[(t - 1, ICH - 2 + ic)])]
                if t == 0 and src is P.xin:
                    w.append((P.c_ugc[ic // 8], 32))
                S.emit("sync", lambda e, ic=ic, par=par: e.dma_start(
                    pug[par][:, :], P.wugck[ic]),
                    waits=w, inc=wld[par], inc_n=16)

                uB, gB = psB[0 + 2 * par], psB[1 + 2 * par]
                w = [(wld[par], wld[par].n), (sc, h128e[(t, 2 * NCH - 1)])]
                if (t, ic - 2) in stt_last:
                    w.append((vc, stt_last[(t, ic - 2)]))
                else:
                    w.append((vc, bfly[(t, 1)]))

                def fillfn(e, par=par, uB=uB, gB=gB):
                    ins = None
                    for off, bk in ((0, uB), (DM, gB)):
                        for kc in range(NCH):
                            ins = e.matmul(bk[:, :],
                                           pug[par][:, off + kc * 128:
                                                    off + (kc + 1) * 128],
                                           yh[:, kc * T:(kc + 1) * T],
                                           start=(kc == 0),
                                           stop=(kc == NCH - 1))
                    return ins
                S.emit("tensor", fillfn, waits=w, inc=mm)
                fillA[(t, ic)] = mm.n

                gs = gscr[ic % 2]
                sw = [(mm, mm.n)]
                if (t, ic - 2) in stt_last:
                    sw.append((vc, stt_last[(t, ic - 2)]))
                S.emit("scalar", lambda e, ic=ic, gB=gB, gs=gs: e.activation(
                    gs[:, :], gB[:, :], AF.Silu,
                    scale=P.sg_sb[:, ic:ic + 1]),
                    waits=sw, inc=sc)
                silu_cnt[(t, ic)] = sc.n

                S.emit("vector", lambda e, ic=ic, uB=uB, gs=gs:
                       e.scalar_tensor_tensor(
                           hbuf[:, ic * T:(ic + 1) * T], uB[:, :],
                           P.al_sb[:, ic:ic + 1], gs[:, :], ALU.mult, ALU.mult),
                       waits=[(sc, sc.n)], inc=vc)
                stt_last[(t, ic)] = vc.n

        def phaseB(t, nxt_steps):
            tc0 = col0 + t * T
            inter = {1: 6, 2: 7, 3: 8, 5: 9, 7: 10, 9: 11, 11: 12}
            for m in range(NCH):
                if m in inter and nxt_steps:
                    nxt_steps[inter[m]]()
                par = m % 2
                dB = psB[4 + par]
                for hf in range(2):
                    w = []
                    if (t, m - 1, hf) in grpB:
                        w = [(mm, grpB[(t, m - 1, hf)])]
                    elif t > 0:
                        w = [(mm, grpB[(t - 1, NCH - 1, hf)])]
                    S.emit("scalar", lambda e, m=m, hf=hf: e.dma_start(
                        dpnh[hf][:, :],
                        P.wdck[m][:, hf * 4096:(hf + 1) * 4096]),
                        waits=w, inc=dld[hf], inc_n=16)
                w2 = []
                if (t, m - 2) in pb_stt:
                    w2 = [(vc, pb_stt[(t, m - 2)])]
                elif t > 0:
                    w2 = [(vc, pb_stt[(t - 1, NCH - 2 + m)])]
                S.emit("sync", lambda e, m=m, par=par, tc0=tc0: e.dma_start(
                    xm[par][:, :], src[m * 128:(m + 1) * 128, tc0:tc0 + T]),
                    waits=w2, inc=xml[par], inc_n=16)

                for hf in range(2):
                    w = [(dld[hf], dld[hf].n)]
                    if hf == 0:
                        if (t, m - 2) in pb_stt:
                            w.append((vc, pb_stt[(t, m - 2)]))
                        elif t > 0:
                            w.append((vc, pb_stt[(t - 1, NCH - 2 + m)]))
                        if m == 0:
                            w.append((vc, stt_last[(t, ICH - 1)]))

                    def dfill(e, par=par, hf=hf, dB=dB):
                        ins = None
                        for j in range(32):
                            kc = hf * 32 + j
                            ins = e.matmul(
                                dB[:, :], dpnh[hf][:, j * 128:(j + 1) * 128],
                                hbuf[:, kc * T:(kc + 1) * T],
                                start=(kc == 0), stop=(kc == ICH - 1))
                        return ins
                    S.emit("tensor", dfill, waits=w, inc=mm)
                    grpB[(t, m, hf)] = mm.n

                w = [(mm, mm.n), (xml[par], xml[par].n), (stb[par], stb[par].n)]
                S.emit("vector", lambda e, m=m, par=par, dB=dB:
                       e.scalar_tensor_tensor(
                           ob[par][:, :], dB[:, :], P.bdp_sb[:, m:m + 1],
                           xm[par][:, :], ALU.mult, ALU.add),
                       waits=w, inc=vc)
                pb_stt[(t, m)] = vc.n

                S.emit("sync", lambda e, m=m, par=par, tc0=tc0: e.dma_start(
                    dst[m * 128:(m + 1) * 128,
                        tc0 + dst_col_off: tc0 + dst_col_off + T],
                    ob[par][:, :]),
                    waits=[(vc, vc.n)], inc=stb[par], inc_n=16)

        st0 = ln_steps(0)
        for f in st0:
            f()
        for t in range(ntiles):
            nxt = ln_steps(t + 1) if t + 1 < ntiles else None
            phaseA(t, nxt)
            phaseB(t, nxt)

        S.emit("sync", None, waits=[(stb[0], stb[0].n), (stb[1], stb[1].n)])
        with nc.Block(tag) as blk:
            S.run_block(blk)

def build_attn(P, a1_only=False, serial_ln=False, no_maskmm=False,
               no_psumexp=False, no_p2=False, no_wo=False, s_only=False,
               no_p1c=False, s_extra=0):
    nc = P.nc
    S = Sched(nc)
    with ExitStack() as ctx:
        E = ctx.enter_context
        ld = Sem(ctx, nc, "a_ld")
        mm = Sem(ctx, nc, "a_mm")
        sc = Sem(ctx, nc, "a_sc")
        vc = Sem(ctx, nc, "a_vc")
        wld = [Sem(ctx, nc, f"a_wld{i}") for i in range(2)]
        xml = [Sem(ctx, nc, f"a_xml{i}") for i in range(2)]
        stb = [Sem(ctx, nc, f"a_stb{i}") for i in range(2)]
        csl = Sem(ctx, nc, "a_csl")

        raw = E(nc.sbuf_tensor("a_raw", [128, 33280], BF16))
        xt = carve(raw, 0, 128, NCH * T, F32)
        xbf = carve(raw, 32768, 128, NCH * T, BF16)
        cs = {}
        for i, nm in enumerate(("cq", "sq", "ck", "sk")):
            cs[nm] = carve(raw, 49152 + i * 2048, 128, T, F32)
        rsc = carve(raw, 57344, 128, T, F32)
        rsc2 = carve(raw, 59392, 128, T, F32)
        vtmp = carve(raw, 61440, 128, T, BF16)
        mrow = carve(raw, 62464, 1, T, F32)
        vrow = carve(raw, 64512, 1, T, F32)
        escr = [carve(raw, 0, 128, 640, F32), carve(raw, 2560, 128, 640, F32)]
        pb = [carve(raw, 5120, 128, 640, BF16), carve(raw, 6400, 128, 640, BF16)]
        pt = [carve(raw, 7680, 128, 640, BF16), carve(raw, 8960, 128, 640, BF16)]
        atile = carve(raw, 10240, 128, NH * T, BF16)
        xm = [carve(raw, 26624, 128, T, F32), carve(raw, 28672, 128, T, F32)]
        ob = [carve(raw, 30720, 128, T, F32), carve(raw, 32768, 128, T, F32)]

        ya = E(nc.sbuf_tensor("a_ya", [128, NCH * T], BF16))
        qbuf = E(nc.sbuf_tensor("a_q", [128, NH * TOWN], BF16))
        kbuf = E(nc.sbuf_tensor("a_k", [128, NKV * TLOC], BF16))
        vtb = E(nc.sbuf_tensor("a_vt", [128, NKV * TLOC], BF16))
        wpan = [E(nc.sbuf_tensor(f"a_wp{i}", [128, DM], BF16)) for i in range(2)]
        krw = E(nc.sbuf_tensor("a_krw", [1, TLOC], BF16))
        bmk = E(nc.sbuf_tensor("a_bmk", [128, 640], BF16))
        nmx = [E(nc.sbuf_tensor(f"a_nm{i}", [128, 1], F32)) for i in range(2)]
        ssm = [E(nc.sbuf_tensor(f"a_ss{i}", [128, 1], F32)) for i in range(2)]
        rcb = [E(nc.sbuf_tensor(f"a_rc{i}", [128, 1], F32)) for i in range(2)]

        psSab = [E(nc.psum_tensor(f"a_psS{i}", [128, 640], F32)) for i in range(2)]
        psO = [E(nc.psum_tensor(f"a_psO{i}", [128, 512], F32)) for i in range(2)]
        psT = E(nc.psum_tensor("a_psT", [128, 640], BF16))
        psX = E(nc.psum_tensor("a_psX", [128, 512], F32))
        psM = psSab[0][:, 0:512]

        S.emit("sync", lambda e: e.dma_start(krw[:, :], P.krow), inc=ld, inc_n=16)
        S.emit("sync", lambda e: e.dma_start(bmk[:, :], P.bmsk), inc=ld, inc_n=16)
        base_ld = ld.n

        stt_ln = {}
        xmsub = {}
        qkv_fill = {}
        slot_cons = {}
        rope_done = {}
        nocs = {}

        def ln_steps(t):
            tc0 = t * T
            srcv = P.xcur[:, tc0:tc0 + T].rearrange("(kc p) t -> p kc t", p=128)
            steps = []

            def s_load():
                w = [(vc, stt_ln.get(t - 1, 0))] if t > 0 else [(ld, base_ld)]
                S.emit("sync", lambda e: e.dma_start(
                    _chunked(xt[:, :], T), srcv), waits=w, inc=ld, inc_n=16)
                xl = ld.n
                S.emit("scalar", lambda e: e.activation(
                    xbf[:, :], xt[:, :], AF.Copy), waits=[(ld, xl)], inc=sc)
            steps.append(s_load)

            def mk_s1():
                def f(e):
                    ins = None
                    for c in range(NCH):
                        ins = e.matmul(psX[0:1, 0:T], P.onk[:, :],
                                       xbf[:, c * T:(c + 1) * T],
                                       start=(c == 0), stop=(c == NCH - 1))
                    return ins
                return f

            def s_s1():
                S.emit("tensor", mk_s1(),
                       waits=[(sc, sc.n), (vc, stt_ln.get(t - 1, 0))], inc=mm)
                S.emit("scalar", lambda e: e.activation(
                    mrow[0:1, 0:T], psX[0:1, 0:T], AF.Copy, scale=1.0 / DM),
                    waits=[(mm, mm.n)], inc=sc)
            steps.append(s_s1)

            def s_mb():
                w = [(sc, sc.n), (vc, xmsub.get(t - 1, 0))]
                S.emit("tensor", lambda e: e.matmul(
                    psM, P.onr[0:1, :], mrow[0:1, 0:T],
                    start=True, stop=True), waits=w, inc=mm)
                mb_mm = mm.n

                def f(e):
                    ins = None
                    for c in range(NCH):
                        sl = slice(c * T, (c + 1) * T)
                        ins = e.tensor_tensor(xt[:, sl], xt[:, sl], psM,
                                              ALU.subtract)
                    return ins
                S.emit("vector", f, waits=[(mm, mb_mm)], inc=vc)
                xmsub[t] = vc.n
                S.emit("scalar", lambda e: e.activation(
                    xbf[:, :], xt[:, :], AF.Square),
                    waits=[(vc, vc.n)], inc=sc)
            steps.append(s_mb)

            def s_s2():
                S.emit("tensor", mk_s1(), waits=[(sc, sc.n)], inc=mm)
                s2_mm = mm.n

                def f(e):
                    e.activation(vrow[0:1, 0:T], psX[0:1, 0:T], AF.Copy,
                                 scale=1.0 / DM)
                    e.activation(vrow[0:1, 0:T], vrow[0:1, 0:T], AF.Ln,
                                 bias=P.fc_sb[0:1, 0:1])
                    return e.activation(vrow[0:1, 0:T], vrow[0:1, 0:T],
                                        AF.Exp, scale=-0.5)
                S.emit("scalar", f, waits=[(mm, s2_mm)], inc=sc)
            steps.append(s_s2)

            def s_rb():
                S.emit("tensor", lambda e: e.matmul(
                    psX[:, 0:T], P.onr[0:1, :], vrow[0:1, 0:T],
                    start=True, stop=True), waits=[(sc, sc.n)], inc=mm)
            steps.append(s_rb)

            def s_stt():
                w = [(mm, mm.n)]
                if t > 0:
                    w.append((mm, qkv_fill[(t - 1, nocs[t - 1] - 1)]))

                def f(e):
                    ins = None
                    for c in range(NCH):
                        ins = e.scalar_tensor_tensor(
                            ya[:, c * T:(c + 1) * T], xt[:, c * T:(c + 1) * T],
                            1.0, psX[:, 0:T], ALU.mult, ALU.mult)
                    return ins
                S.emit("vector", f, waits=w, inc=vc)
                stt_ln[t] = vc.n
            steps.append(s_stt)

            def s_cs():
                for nm, srcap in (("cq", P.cosq), ("sq", P.sinq),
                                  ("ck", P.cosk), ("sk", P.sink)):
                    w = [(vc, rope_done.get(t - 1, 0))]
                    S.emit("sync", lambda e, nm=nm, srcap=srcap: e.dma_start(
                        cs[nm][:, :], srcap[:, tc0:tc0 + T]), waits=w,
                        inc=csl, inc_n=16)
            steps.append(s_cs)
            return steps

        for t in range(5):
            tc0 = t * T
            has_q = tc0 + T > HALO
            ocs = []
            if has_q:
                for h in range(NH):
                    ocs.append(("q", h, P.wqck))
            for h in range(NKV):
                ocs.append(("k", h, P.wkck))
            for h in range(NKV):
                ocs.append(("v", h, P.wvck))
            nocs[t] = len(ocs)

            if t == 0:
                for f in ln_steps(0):
                    f()
            if serial_ln and t > 0:
                for f in ln_steps(t):
                    f()
            nxt = None if serial_ln else (ln_steps(t + 1) if t < 4 else None)
            n_oc = len(ocs)
            inter = {}
            if nxt:
                step_pos = [1, 4, 7, 10, 13] if n_oc >= 15 else [1, 2, 3, 4, 5]
                for j, p_ in enumerate(step_pos):
                    inter[p_] = j

            qkvB = [psO[0][:, :], psO[1][:, :], psSab[1][:, 0:512]]
            for oi, (kind, h, wsrc) in enumerate(ocs):
                if nxt and oi in inter:
                    nxt[inter[oi]]()
                par = oi % 2
                bank = qkvB[oi % 3]
                w = []
                if (t, oi - 2) in qkv_fill:
                    w = [(mm, qkv_fill[(t, oi - 2)])]
                elif t > 0:
                    w = [(mm, qkv_fill[(t - 1, nocs[t - 1] - 2 + oi)])]
                else:
                    w = [(P.c_at, P.c_at.n)]
                S.emit("scalar", lambda e, h=h, wsrc=wsrc, par=par: e.dma_start(
                    wpan[par][:, :], wsrc[h]),
                    waits=w, inc=wld[par], inc_n=16)

                w = [(wld[par], wld[par].n), (vc, stt_ln[t])]
                pc = slot_cons.get((t, oi - 3))
                if pc is None and t > 0:
                    pc = slot_cons.get((t - 1, nocs[t - 1] - 3 + oi))
                if pc is not None:
                    w.append((pc[0], pc[1]))

                def qf(e, par=par, bank=bank):
                    ins = None
                    for kc in range(NCH):
                        ins = e.matmul(bank[:, :],
                                       wpan[par][:, kc * 128:(kc + 1) * 128],
                                       ya[:, kc * T:(kc + 1) * T],
                                       start=(kc == 0), stop=(kc == NCH - 1))
                    return ins
                S.emit("tensor", qf, waits=w, inc=mm)
                qkv_fill[(t, oi)] = mm.n

                if kind in ("q", "k"):
                    co, so = (cs["cq"], cs["sq"]) if kind == "q" else                              (cs["ck"], cs["sk"])
                    if kind == "q":
                        dest = qbuf[:, h * TOWN + (tc0 - HALO):
                                    h * TOWN + (tc0 - HALO) + T]
                    else:
                        dest = kbuf[:, h * TLOC + tc0: h * TLOC + tc0 + T]

                    def rope(e, bank=bank, co=co, so=so, dest=dest):
                        e.tensor_tensor(rsc[:, :], bank[:, :], co[:, :],
                                        ALU.mult)
                        e.tensor_tensor(rsc2[0:64, :], bank[64:128, :],
                                        so[0:64, :], ALU.mult)
                        e.tensor_tensor(rsc2[64:128, :], bank[0:64, :],
                                        so[64:128, :], ALU.mult)
                        return e.tensor_tensor(dest, rsc[:, :], rsc2[:, :],
                                               ALU.add)
                    S.emit("vector", rope, waits=[(mm, mm.n), (csl, csl.n)],
                           inc=vc)
                    rope_done[t] = vc.n
                    slot_cons[(t, oi)] = (vc, vc.n)
                else:
                    S.emit("scalar", lambda e, bank=bank: e.activation(
                        vtmp[:, :], bank[:, :], AF.Copy),
                        waits=[(mm, mm.n)], inc=sc)
                    slot_cons[(t, oi)] = (sc, sc.n)
                    for b in range(4):
                        S.emit("tensor", lambda e, b=b: e.transpose(
                            psT[:, 0:128], vtmp[:, b * 128:(b + 1) * 128],
                            P.idn[:, :]),
                            waits=[(sc, sc.n)], inc=mm)
                        blk_i = tc0 // 128 + b
                        S.emit("scalar",
                               lambda e, h=h, blk_i=blk_i: e.activation(
                                   vtb[:, (h * 20 + blk_i) * 128:
                                       (h * 20 + blk_i) * 128 + 128],
                                   psT[:, 0:128], AF.Copy),
                               waits=[(mm, mm.n)], inc=sc)
            if nxt:
                nxt[5]()
                nxt[6]()
        a1_vc = vc.n
        a1_sc = sc.n
        a1_mm = mm.n

        if a1_only:
            S.emit("sync", None, waits=[(vc, a1_vc), (sc, a1_sc), (mm, a1_mm),
                                        (csl, csl.n)])
            with nc.Block("attn") as blk:
                S.run_block(blk)
            return

        NIT = 4 * NH
        sfill = {}
        nmx_v = {}
        exp_s = {}
        sum_v = {}
        psc_v = {}
        tr_m = {}
        pte_s = {}
        pv_m = {}
        oev_s = {}
        wo_fill = {}
        wo_stt = {}

        for wt in range(4):
            t0own = wt * T
            iters = [(t0own // 128 + qb4, h) for qb4 in range(4)
                     for h in range(NH)]

            def kprev(d, it, wt=wt):
                if it >= 0:
                    return d.get((wt, it))
                if wt > 0:
                    return d.get((wt - 1, NIT + it))
                return None

            def emit_S(it, wt=wt, iters=iters):
                i, h = iters[it]
                par = it % 2
                sAB = psSab[par]
                kv = h // NKV
                qsl = qbuf[:, h * TOWN + i * 128: h * TOWN + i * 128 + 128]
                w = []
                if s_only:
                    pv = kprev(nmx_v, it - 2)
                    if pv is not None:
                        w.append((vc, pv))
                    else:
                        w.append((vc, a1_vc))
                        w.append((sc, a1_sc))
                else:
                    pv = kprev(exp_s, it - 2)
                    if pv is not None:
                        w.append((sc, pv))
                    else:
                        w.append((vc, a1_vc))
                        w.append((sc, a1_sc))
                if no_p1c:
                    pv = kprev(sum_v, it - 2)
                    if pv is not None:
                        w.append((vc, pv))

                def f(e, i=i, kv=kv, sAB=sAB, qsl=qsl):
                    e.matmul(sAB[:, 0:512], P.onb[0:1, :],
                             krw[0:1, i * 128: i * 128 + 512],
                             start=True, stop=False)
                    if not no_maskmm:
                        e.matmul(sAB[:, 0:512], P.idn[:, :], bmk[:, 0:512],
                                 start=False, stop=False)
                    e.matmul(sAB[:, 0:512], qsl,
                             kbuf[:, kv * TLOC + i * 128:
                                  kv * TLOC + i * 128 + 512],
                             start=False, stop=True)
                    if not no_maskmm:
                        e.matmul(sAB[:, 512:640], P.idn[:, :], bmk[:, 512:640],
                                 start=True, stop=False)
                        return e.matmul(sAB[:, 512:640], qsl,
                                        kbuf[:, kv * TLOC + i * 128 + 512:
                                             kv * TLOC + i * 128 + 640],
                                        start=False, stop=True)
                    return e.matmul(sAB[:, 512:640], qsl,
                                    kbuf[:, kv * TLOC + i * 128 + 512:
                                         kv * TLOC + i * 128 + 640],
                                    start=True, stop=True)
                S.emit("tensor", f, waits=w, inc=mm)
                sfill[(wt, it)] = mm.n

            def p1a(it, wt=wt):
                par = it % 2
                w = [(mm, sfill[(wt, it)])]
                pv = kprev(exp_s, it - 2)
                if pv is not None:
                    w.append((sc, pv))
                if s_only:
                    S.emit("vector", lambda e, par=par: e.tensor_copy(
                        escr[par][:, :], psSab[par][:, :]), waits=w, inc=vc)
                    if s_extra == 1:
                        S.emit("vector", lambda e, par=par: e.tensor_reduce(
                            nmx[par][:, 0:1], psSab[par][:, :], AX.X, ALU.max,
                            negate=True), inc=vc)
                    elif s_extra == 2:
                        S.emit("vector", lambda e, par=par: e.tensor_reduce(
                            nmx[par][:, 0:1], escr[par][:, :], AX.X, ALU.max,
                            negate=True), inc=vc)
                    elif s_extra == 3:
                        S.emit("scalar", lambda e, par=par: e.activation(
                            pb[par][:, :], escr[par][:, :], AF.Exp), inc=sc)
                    elif s_extra == 6:
                        S.emit("vector", lambda e, par=par: e.tensor_reduce(
                            nmx[par][:, 0:1], escr[par][:, :], AX.X, ALU.max,
                            negate=True), inc=vc)
                        S.emit("scalar", lambda e, par=par: e.activation(
                            pb[par][:, :], escr[par][:, :], AF.Exp,
                            bias=nmx[par][:, 0:1]),
                            waits=[(vc, vc.n)], inc=sc)
                    elif s_extra == 7:
                        S.emit("vector", lambda e, par=par: e.tensor_reduce(
                            nmx[par][:, 0:1], escr[par][:, :], AX.X, ALU.max,
                            negate=True), inc=vc)
                        S.emit("scalar", lambda e, par=par: e.activation(
                            pb[par][:, :], escr[par][:, :], AF.Exp,
                            bias=nmx[par][:, 0:1]),
                            waits=[(vc, vc.n)], inc=sc)
                        S.emit("vector", lambda e, par=par: e.tensor_reduce(
                            ssm[par][:, 0:1], pb[par][:, :], AX.X, ALU.add),
                            waits=[(sc, sc.n)], inc=vc)
                    elif s_extra == 4:
                        S.emit("vector", lambda e, par=par: e.tensor_reduce(
                            nmx[par][:, 0:1], psSab[par][:, :], AX.X, ALU.max,
                            negate=True), inc=vc)
                        S.emit("scalar", lambda e, par=par: e.activation(
                            escr[par][:, :], psSab[par][:, :], AF.Exp,
                            bias=nmx[par][:, 0:1]),
                            waits=[(vc, vc.n)], inc=sc)
                    nmx_v[(wt, it)] = vc.n
                    return
                if no_psumexp:
                    pv2 = kprev(psc_v, it - 2)
                    if pv2 is not None:
                        w.append((vc, pv2))
                    S.emit("vector", lambda e, par=par: e.tensor_copy(
                        escr[par][:, :], psSab[par][:, :]), waits=w, inc=vc)
                    S.emit("vector", lambda e, par=par: e.tensor_reduce(
                        nmx[par][:, 0:1], escr[par][:, :], AX.X, ALU.max,
                        negate=True), inc=vc)
                else:
                    S.emit("vector", lambda e, par=par: e.tensor_reduce(
                        nmx[par][:, 0:1], psSab[par][:, :], AX.X, ALU.max,
                        negate=True),
                        waits=w, inc=vc)
                nmx_v[(wt, it)] = vc.n

            def p1b(it, wt=wt):
                par = it % 2
                w = [(vc, nmx_v[(wt, it)])]
                pv = kprev(psc_v, it - 2)
                if pv is not None:
                    w.append((vc, pv))
                if no_psumexp:
                    S.emit("scalar", lambda e, par=par: e.activation(
                        escr[par][:, :], escr[par][:, :], AF.Exp,
                        bias=nmx[par][:, 0:1]),
                        waits=w, inc=sc)
                else:
                    S.emit("scalar", lambda e, par=par: e.activation(
                        escr[par][:, :], psSab[par][:, :], AF.Exp,
                        bias=nmx[par][:, 0:1]),
                        waits=w, inc=sc)
                exp_s[(wt, it)] = sc.n
                S.emit("vector", lambda e, par=par: e.tensor_reduce(
                    ssm[par][:, 0:1], escr[par][:, :], AX.X, ALU.add),
                    waits=[(sc, sc.n)], inc=vc)
                sum_v[(wt, it)] = vc.n

            def p1c(it, wt=wt):
                par = it % 2

                def rcp(e, par=par):
                    e.activation(rcb[par][:, 0:1], ssm[par][:, 0:1], AF.Ln)
                    return e.activation(rcb[par][:, 0:1], rcb[par][:, 0:1],
                                        AF.Exp, scale=-1.0)
                S.emit("scalar", rcp, waits=[(vc, sum_v[(wt, it)])], inc=sc)
                w = [(sc, sc.n)]
                pv = kprev(tr_m, it - 2)
                if pv is not None:
                    w.append((mm, pv))
                S.emit("vector", lambda e, par=par: e.tensor_scalar_mul(
                    pb[par][:, :], escr[par][:, :], rcb[par][:, 0:1]),
                    waits=w, inc=vc)
                psc_v[(wt, it)] = vc.n

            def p2(it, wt=wt, iters=iters):
                i, h = iters[it]
                par = it % 2
                kv = h // NKV
                qb4 = i - (wt * 4)
                w = [(vc, psc_v[(wt, it)])]
                pv = kprev(pte_s, it - 1)
                if pv is not None:
                    w.append((sc, pv))

                def tr(e, par=par):
                    ins = None
                    for j in range(5):
                        ins = e.transpose(psT[:, j * 128:(j + 1) * 128],
                                          pb[par][:, j * 128:(j + 1) * 128],
                                          P.idn[:, :])
                    return ins
                S.emit("tensor", tr, waits=w, inc=mm)
                tr_m[(wt, it)] = mm.n
                w = [(mm, mm.n)]
                pv = kprev(pv_m, it - 2)
                if pv is not None:
                    w.append((mm, pv))
                S.emit("scalar", lambda e, par=par: e.activation(
                    pt[par][:, :], psT[:, :], AF.Copy),
                    waits=w, inc=sc)
                pte_s[(wt, it)] = sc.n

                w = [(sc, sc.n)]
                if it >= 2:
                    w.append((sc, oev_s[(wt, it - 2)]))
                elif wt > 0:
                    w.append((vc, wo_stt[(wt - 1, NCH - 2 + it)]))

                def pvf(e, i=i, kv=kv, par=par):
                    ins = None
                    for j in range(5):
                        blk_i = i + j
                        ins = e.matmul(psO[par][:, 0:128],
                                       vtb[:, (kv * 20 + blk_i) * 128:
                                           (kv * 20 + blk_i) * 128 + 128],
                                       pt[par][:, j * 128:(j + 1) * 128],
                                       start=(j == 0), stop=(j == 4))
                    return ins
                S.emit("tensor", pvf, waits=w, inc=mm)
                pv_m[(wt, it)] = mm.n
                S.emit("scalar", lambda e, h=h, qb4=qb4, par=par: e.activation(
                    atile[:, h * T + qb4 * 128: h * T + qb4 * 128 + 128],
                    psO[par][:, 0:128], AF.Copy),
                    waits=[(mm, mm.n)], inc=sc)
                oev_s[(wt, it)] = sc.n

            for i in range(NIT + 4):
                if 1 <= i <= NIT:
                    p1a(i - 1)
                if 2 <= i <= NIT + 1 and not s_only:
                    p1b(i - 2)
                if 3 <= i <= NIT + 2 and not s_only and not no_p1c:
                    p1c(i - 3)
                if 4 <= i and not no_p2:
                    p2(i - 4)
                if i < NIT:
                    emit_S(i)
            if no_p2 or no_wo:
                if wt == 3:
                    S.emit("sync", None,
                           waits=[(vc, vc.n), (sc, sc.n), (mm, mm.n)])
                    with nc.Block("attn") as blk:
                        S.run_block(blk)
                    return
                continue
            a_done_sc = oev_s[(wt, NIT - 1)]

            for m in range(NCH):
                par = m % 2
                w = []
                if (wt, m - 2) in wo_fill:
                    w = [(mm, wo_fill[(wt, m - 2)])]
                elif wt > 0:
                    w = [(mm, wo_fill[(wt - 1, NCH - 2 + m)])]
                else:
                    w = [(mm, a1_mm)]
                S.emit("scalar", lambda e, m=m, par=par: e.dma_start(
                    wpan[par][:, :], P.wock[m]),
                    waits=w, inc=wld[par], inc_n=16)
                w2 = []
                if (wt, m - 2) in wo_stt:
                    w2 = [(vc, wo_stt[(wt, m - 2)])]
                elif wt > 0:
                    w2 = [(vc, wo_stt[(wt - 1, NCH - 2 + m)])]
                else:
                    w2 = [(mm, a1_mm), (vc, a1_vc), (sc, a1_sc)]
                S.emit("sync", lambda e, m=m, par=par, t0own=t0own: e.dma_start(
                    xm[par][:, :],
                    P.xcur[m * 128:(m + 1) * 128,
                           HALO + t0own: HALO + t0own + T]),
                    waits=w2, inc=xml[par], inc_n=16)

                w = [(wld[par], wld[par].n), (sc, a_done_sc)]
                if (wt, m - 2) in wo_stt:
                    w.append((vc, wo_stt[(wt, m - 2)]))
                elif wt > 0 and m < 2:
                    w.append((vc, wo_stt[(wt - 1, NCH - 2 + m)]))

                def wof(e, par=par):
                    ins = None
                    for kc in range(NCH):
                        ins = e.matmul(psO[par][:, :],
                                       wpan[par][:, kc * 128:(kc + 1) * 128],
                                       atile[:, kc * T:(kc + 1) * T],
                                       start=(kc == 0), stop=(kc == NCH - 1))
                    return ins
                S.emit("tensor", wof, waits=w, inc=mm)
                wo_fill[(wt, m)] = mm.n
                w = [(mm, mm.n), (xml[par], xml[par].n), (stb[par], stb[par].n)]
                S.emit("vector", lambda e, par=par: e.tensor_tensor(
                    ob[par][:, :], psO[par][:, :], xm[par][:, :], ALU.add),
                    waits=w, inc=vc)
                wo_stt[(wt, m)] = vc.n
                S.emit("sync", lambda e, m=m, par=par, t0own=t0own: e.dma_start(
                    P.xcur[m * 128:(m + 1) * 128,
                           HALO + t0own: HALO + t0own + T],
                    ob[par][:, :]),
                    waits=[(vc, vc.n)], inc=stb[par], inc_n=16)

        S.emit("sync", None, waits=[(stb[0], stb[0].n), (stb[1], stb[1].n)])
        with nc.Block("attn") as blk:
            S.run_block(blk)


def _host_inputs(L0_attn_x, L0_attn_Wq, L0_attn_Wk, L0_attn_Wv, L0_attn_Wo,
                 L0_ffn_SU_ug, L0_ffn_SU_dp, L0_ffn_Wscale_ug, L0_ffn_Wscale_dp,
                 L0_ffn_W_up, L0_ffn_W_gate, L0_ffn_W_down):
    bf = ml_dtypes.bfloat16
    f32 = np.float32

    def blockw(W, npan):
        W = np.asarray(W, f32)
        o, i = W.shape
        return np.ascontiguousarray(
            W.reshape(npan, 128, i // 128, 128).transpose(0, 3, 2, 1))

    shared = {
        "wupB": blockw(L0_ffn_W_up, ICH),
        "wgB": blockw(L0_ffn_W_gate, ICH),
        "wdB": blockw(L0_ffn_W_down, NCH),
        "wqB": blockw(L0_attn_Wq, NH),
        "wkB": blockw(L0_attn_Wk, NKV),
        "wvB": blockw(L0_attn_Wv, NKV),
        "woB": blockw(L0_attn_Wo, NCH),
        "su_pc": np.ascontiguousarray(
            np.asarray(L0_ffn_SU_ug, f32).reshape(NCH, 128).T),
        "wsu_pc": np.ascontiguousarray(
            np.asarray(L0_ffn_Wscale_ug, f32)[:INTER].reshape(ICH, 128).T),
        "wsg_pc": np.ascontiguousarray(
            np.asarray(L0_ffn_Wscale_ug, f32)[INTER:].reshape(ICH, 128).T),
        "sudp_pc": np.ascontiguousarray(
            np.asarray(L0_ffn_SU_dp, f32).reshape(ICH, 128).T),
        "wsdp_pc": np.ascontiguousarray(
            np.asarray(L0_ffn_Wscale_dp, f32).reshape(NCH, 128).T),
    }
    ij = np.arange(128)
    pop = np.zeros((128, 128), np.int64)
    for b in range(7):
        pop += ((ij[:, None] >> b) & 1) * ((ij[None, :] >> b) & 1)
    shared["h128"] = np.where(pop % 2 == 0, 1.0, -1.0).astype(bf)
    shared["iden"] = np.eye(128).astype(bf)
    shared["onesk"] = np.ones((128, 1), bf)
    shared["onesr"] = np.ones((1, 128), f32)
    shared["onesb"] = np.ones((1, 128), bf)
    p = np.arange(128)[:, None]
    x = np.arange(640)[None, :]
    valid = np.where(x < 512, x > p, (x - 512) <= p)
    shared["bmsk"] = np.where(valid, 0.0, NEG).astype(bf)
    shared["fconst"] = np.array([[EPS, 0.0]], f32)

    x4 = np.asarray(L0_attn_x, f32)
    scale = 1.0 / math.sqrt(DH)
    inv_freq = (10000.0 ** (-np.arange(0, DH, 2, dtype=np.float64) / DH))

    in_maps = []
    for core in range(8):
        b, half = divmod(core, 2)
        if half == 0:
            xloc = np.concatenate(
                [np.zeros((HALO, DM), f32), x4[b, 0:TOWN]], axis=0)
            abspos = np.arange(-HALO, TOWN)
            krow = np.concatenate(
                [np.full(HALO, NEG, f32), np.zeros(TOWN, f32)])
        else:
            xloc = x4[b, TOWN - HALO: 4096]
            abspos = np.arange(TOWN - HALO, 4096)
            krow = np.zeros(TLOC, f32)
        pos = np.maximum(abspos, 0)
        ang = pos[:, None].astype(np.float64) * inv_freq[None, :]
        c64 = np.cos(ang).astype(f32)
        s64 = np.sin(ang).astype(f32)
        cosf = np.concatenate([c64, c64], axis=1).T
        sinf = np.concatenate([-s64, s64], axis=1).T
        m = dict(shared)
        m["xin"] = np.ascontiguousarray(xloc.T)
        m["cosq"] = np.ascontiguousarray(cosf * scale).astype(f32)
        m["sinq"] = np.ascontiguousarray(sinf * scale).astype(f32)
        m["cosk"] = np.ascontiguousarray(cosf)
        m["sink"] = np.ascontiguousarray(sinf)
        m["krow"] = krow[None, :].astype(bf)
        in_maps.append(m)
    return in_maps


def kernel(**inputs):
    in_maps = _host_inputs(**inputs)
    nc = build_program()
    res = run_bass_kernel_spmd(nc, in_maps, core_ids=list(range(8)))
    out = np.empty((4, 4096, DM), np.float32)
    for core in range(8):
        b, half = divmod(core, 2)
        out[b, half * TOWN:(half + 1) * TOWN, :] = res.results[core]["xout"].T
    return out


# version 13
# speedup vs baseline: 1.0922x; 1.0735x over previous
import math
from contextlib import ExitStack

import numpy as np
import ml_dtypes

import concourse.bass as bass
import concourse.mybir as mybir
from concourse.bass_utils import run_bass_kernel_spmd

F32 = mybir.dt.float32
BF16 = mybir.dt.bfloat16
AF = mybir.ActivationFunctionType
ALU = mybir.AluOpType
AX = mybir.AxisListType

DM = 2048
NCH = 16
INTER = 8192
ICH = 64
TLOC = 2560
HALO = 512
TOWN = 2048
NH = 16
NKV = 4
DH = 128
EPS = 1e-5
RS2048 = 1.0 / math.sqrt(2048.0)
RS8192 = 1.0 / math.sqrt(8192.0)
NEG = -1e9
T = 512

ENGINES = ("sync", "gpsimd", "tensor", "scalar", "vector")


class Sem:
    def __init__(self, ctx, nc, name):
        self.h = ctx.enter_context(nc.semaphore(name))
        self.n = 0


class Sched:
    def __init__(self, nc):
        self.nc = nc
        self.q = {e: [] for e in ENGINES}

    def emit(self, eng, fn, waits=(), inc=None, inc_n=1):
        w = tuple((s.h, v) for (s, v) in waits if v > 0)
        if inc is not None:
            inc.n += inc_n
        self.q[eng].append((w, fn, None if inc is None else inc.h, inc_n))

    def run_block(self, blk):
        for eng in ENGINES:
            items = self.q[eng]
            if not items:
                continue

            def body(e, items=items):
                for waits, fn, inc_h, inc_n in items:
                    for (sem, val) in waits:
                        e.wait_ge(sem, val)
                    ins = fn(e) if fn is not None else None
                    if inc_h is not None:
                        if ins is None:
                            ins = e.nop()
                        ins.then_inc(inc_h, inc_n)

            getattr(blk, eng)(body)
        self.q = {e: [] for e in ENGINES}


class Prog:
    pass


def _chunked(ap2d, t):
    n = ap2d.shape[1] // t
    return ap2d.rearrange("p (n t) -> p n t", n=n, t=t)


def carve(raw, byte_off, p, n, dtype):
    c0 = byte_off // 2
    if dtype == BF16:
        return raw.ap()[0:p, c0:c0 + n]
    return raw.ap()[0:p, c0:c0 + 2 * n].bitcast(F32)


def build_program(passes=("prep", "ffn1", "attn", "ffn3"), xcur_out=False, dbg_scratch=False, detect_races=True, attn_kw=None):
    nc = bass.Bass("TRN2", target_bir_lowering=False,
                   detect_race_conditions=detect_races)
    P = Prog()
    P.nc = nc
    P.ctx = ExitStack()

    dt = nc.dram_tensor
    P.xin = dt("xin", [DM, TLOC], F32, kind="ExternalInput").ap()
    P.wupB = dt("wupB", [ICH, 128, NCH, 128], F32, kind="ExternalInput").ap()
    P.wgB = dt("wgB", [ICH, 128, NCH, 128], F32, kind="ExternalInput").ap()
    P.wdB = dt("wdB", [NCH, 128, ICH, 128], F32, kind="ExternalInput").ap()
    P.wqB = dt("wqB", [NH, 128, NCH, 128], F32, kind="ExternalInput").ap()
    P.wkB = dt("wkB", [NKV, 128, NCH, 128], F32, kind="ExternalInput").ap()
    P.wvB = dt("wvB", [NKV, 128, NCH, 128], F32, kind="ExternalInput").ap()
    P.woB = dt("woB", [NCH, 128, NCH, 128], F32, kind="ExternalInput").ap()
    P.su_pc = dt("su_pc", [128, NCH], F32, kind="ExternalInput").ap()
    P.wsu_pc = dt("wsu_pc", [128, ICH], F32, kind="ExternalInput").ap()
    P.wsg_pc = dt("wsg_pc", [128, ICH], F32, kind="ExternalInput").ap()
    P.sudp_pc = dt("sudp_pc", [128, ICH], F32, kind="ExternalInput").ap()
    P.wsdp_pc = dt("wsdp_pc", [128, NCH], F32, kind="ExternalInput").ap()
    P.h128 = dt("h128", [128, 128], BF16, kind="ExternalInput").ap()
    P.iden = dt("iden", [128, 128], BF16, kind="ExternalInput").ap()
    P.onesk = dt("onesk", [128, 1], BF16, kind="ExternalInput").ap()
    P.onesr = dt("onesr", [1, 128], F32, kind="ExternalInput").ap()
    P.onesb = dt("onesb", [1, 128], BF16, kind="ExternalInput").ap()
    P.cosq = dt("cosq", [128, TLOC], F32, kind="ExternalInput").ap()
    P.sinq = dt("sinq", [128, TLOC], F32, kind="ExternalInput").ap()
    P.cosk = dt("cosk", [128, TLOC], F32, kind="ExternalInput").ap()
    P.sink = dt("sink", [128, TLOC], F32, kind="ExternalInput").ap()
    P.krow = dt("krow", [1, TLOC], BF16, kind="ExternalInput").ap()
    P.bmsk = dt("bmsk", [128, 640], BF16, kind="ExternalInput").ap()
    P.fconst = dt("fconst", [1, 2], F32, kind="ExternalInput").ap()
    P.xcur = dt("xcur", [DM, TLOC], F32,
                kind="ExternalOutput" if xcur_out else "Internal").ap()
    sk = "ExternalOutput" if dbg_scratch else "Internal"
    P.wdck = dt("wdck", [NCH, 128, INTER], BF16, kind=sk).ap()
    P.wdbf = dt("wdbf", [NCH, 128, INTER], BF16, kind="Internal").ap()
    P.wugck = dt("wugck", [ICH, 128, 2 * DM], BF16, kind=sk).ap()
    P.wqck = dt("wqck", [NH, 128, DM], BF16, kind="Internal").ap()
    P.wkck = dt("wkck", [NKV, 128, DM], BF16, kind="Internal").ap()
    P.wvck = dt("wvck", [NKV, 128, DM], BF16, kind="Internal").ap()
    P.wock = dt("wock", [NCH, 128, DM], BF16, kind="Internal").ap()
    P.xout = dt("xout", [DM, TOWN], F32, kind="ExternalOutput").ap()

    E = P.ctx.enter_context
    P.hH = E(nc.sbuf_tensor("hH", [128, 128], BF16))
    P.idn = E(nc.sbuf_tensor("idn", [128, 128], BF16))
    P.onk = E(nc.sbuf_tensor("onk", [128, 1], BF16))
    P.onr = E(nc.sbuf_tensor("onr", [1, 128], F32))
    P.onb = E(nc.sbuf_tensor("onb", [1, 128], BF16))
    P.su_sb = E(nc.sbuf_tensor("su_sb", [128, NCH], F32))
    P.sg_sb = E(nc.sbuf_tensor("sg_sb", [128, ICH], F32))
    P.al_sb = E(nc.sbuf_tensor("al_sb", [128, ICH], F32))
    P.bdp_sb = E(nc.sbuf_tensor("bdp_sb", [128, NCH], F32))
    P.fc_sb = E(nc.sbuf_tensor("fc_sb", [1, 2], F32))
    class _PSem:
        pass
    P.c_ugc = []
    for i in range(8):
        ps = _PSem(); ps.h = E(nc.semaphore(f"c_ug{i}")); ps.n = 0
        P.c_ugc.append(ps)
    P.c_at = _PSem(); P.c_at.h = E(nc.semaphore("c_at")); P.c_at.n = 0

    if "prep" in passes:
        build_prep(P)
    if "ffn1" in passes:
        build_ffn_pass(P, src=P.xin, dst=P.xcur, col0=0, ncols=TLOC, tag="f1")
    if "attn" in passes:
        build_attn(P, **(attn_kw or {}))
    if "ffn3" in passes:
        build_ffn_pass(P, src=P.xcur, dst=P.xout, col0=HALO, ncols=TOWN,
                       tag="f3", dst_col_off=-HALO)
    P.ctx.close()
    return nc


def build_prep(P):
    nc = P.nc
    S = Sched(nc)
    with ExitStack() as ctx:
        E = ctx.enter_context
        ld = Sem(ctx, nc, "p_ld")
        wpl = [Sem(ctx, nc, f"p_wpl{i}") for i in range(2)]
        mm = Sem(ctx, nc, "p_mm")
        sc = Sem(ctx, nc, "p_sc")
        vc = Sem(ctx, nc, "p_vc")
        gp = Sem(ctx, nc, "p_gp")
        c_at = P.c_at
        pso = [Sem(ctx, nc, f"p_pso{i}") for i in range(2)]

        wsu = E(nc.sbuf_tensor("p_wsu", [128, ICH], F32))
        sudp = E(nc.sbuf_tensor("p_sudp", [128, ICH], F32))
        pnl = [E(nc.sbuf_tensor(f"p_pnl{i}", [128, INTER], BF16)) for i in range(2)]
        wk32 = E(nc.sbuf_tensor("p_wk32", [128, INTER], BF16))
        wk32b = E(nc.sbuf_tensor("p_wk32b", [128, INTER], BF16))
        pwk32 = E(nc.sbuf_tensor("p_pwk32", [128, INTER], BF16))
        pwk32b = E(nc.sbuf_tensor("p_pwk32b", [128, INTER], BF16))
        pnlo = [E(nc.sbuf_tensor(f"p_pnlo{i}", [128, INTER], BF16)) for i in range(2)]
        psQ = [E(nc.psum_tensor(f"p_psQ{i}", [128, 512], F32)) for i in range(4)]

        wdc = [Sem(ctx, nc, f"p_wdc{i}") for i in range(NCH)]
        for mp_ in range(NCH):
            S.emit("gpsimd", lambda e, mp_=mp_: e.dma_start(
                P.wdbf[mp_], P.wdB[mp_].rearrange("p c q -> p (c q)")),
                inc=wdc[mp_], inc_n=16)
        up3 = P.wupB.rearrange("c p k m -> c p (k m)")
        g3 = P.wgB.rearrange("c p k m -> c p (k m)")
        for ck in range(8):
            sl = slice(8 * ck, 8 * ck + 8)
            S.emit("gpsimd", lambda e, sl=sl: e.dma_start(
                P.wugck[sl, :, 0:DM], up3[sl]), inc=P.c_ugc[ck], inc_n=16)
            S.emit("gpsimd", lambda e, sl=sl: e.dma_start(
                P.wugck[sl, :, DM:2 * DM], g3[sl]), inc=P.c_ugc[ck], inc_n=16)
        for dstw, srcw in ((P.wqck, P.wqB), (P.wkck, P.wkB),
                           (P.wvck, P.wvB), (P.wock, P.woB)):
            S.emit("gpsimd", lambda e, dstw=dstw, srcw=srcw: e.dma_start(
                dstw.flatten(), srcw.flatten()), inc=c_at, inc_n=16)

        consts = [
            (P.hH[:, :], P.h128), (P.idn[:, :], P.iden),
            (P.onk[:, :], P.onesk), (P.onr[:, :], P.onesr), (P.onb[:, :], P.onesb),
            (P.su_sb[:, :], P.su_pc), (P.sg_sb[:, :], P.wsg_pc),
            (P.bdp_sb[:, :], P.wsdp_pc),
            (wsu[:, :], P.wsu_pc), (sudp[:, :], P.sudp_pc),
            (P.fc_sb[:, :], P.fconst),
        ]
        for d, s in consts:
            S.emit("sync", lambda e, d=d, s=s: e.dma_start(d, s), inc=ld, inc_n=16)
        n_const = ld.n

        S.emit("scalar",
               lambda e: e.activation(P.sg_sb[:, :], P.sg_sb[:, :], AF.Copy,
                                      scale=RS2048),
               waits=[(ld, n_const)])
        S.emit("vector",
               lambda e: e.scalar_tensor_tensor(P.al_sb[:, :], wsu[:, :], RS2048,
                                                sudp[:, :], ALU.mult, ALU.mult),
               waits=[(ld, n_const)], inc=vc)
        S.emit("scalar",
               lambda e: e.activation(P.bdp_sb[:, :], P.bdp_sb[:, :], AF.Copy,
                                      scale=0.5 * RS8192),
               waits=[(ld, n_const)])

        FR = 5440
        pe_done = {}
        s_evac = {}
        bf_v = {}
        for mp in range(NCH):
            par = mp % 2
            wkA, wkB_ = (wk32, wk32b) if par == 0 else (pwk32, pwk32b)
            w = [(mm, pe_done.get(mp - 2, 0)), (wdc[mp], 16)]
            S.emit("scalar",
                   lambda e, mp=mp, par=par: e.dma_start(
                       pnl[par][:, :], P.wdbf[mp]),
                   waits=w, inc=wpl[par], inc_n=16)
            for q in range(16):
                qb = q % 4
                w = [(wpl[par], wpl[par].n)]
                prev = s_evac.get((mp, q - 4), s_evac.get((mp - 1, 12 + q), 0))
                w.append((sc, prev))
                S.emit("tensor",
                       lambda e, q=q, qb=qb, par=par: e.matmul(
                           psQ[qb][:, :], P.hH[:, :],
                           pnl[par][:, 512 * q:512 * (q + 1)],
                           start=True, stop=True),
                       waits=w, inc=mm)
                ew = [(mm, mm.n)]
                if q == 0 and (mp - 2) in bf_v:
                    ew.append((vc, bf_v[mp - 2][0]))
                    ew.append((gp, bf_v[mp - 2][1]))
                S.emit("scalar",
                       lambda e, q=q, qb=qb, wkA=wkA: e.activation(
                           wkA[:, 512 * q:512 * (q + 1)], psQ[qb][:, :],
                           AF.Copy),
                       waits=ew, inc=sc)
                s_evac[(mp, q)] = sc.n
            pe_done[mp] = mm.n
            evac_done = sc.n

            src = wkA
            vstage = 0
            gstage = 0
            for st in range(6):
                dstb = pnlo[par] if st == 5 else (wkB_ if src is wkA else wkA)
                ns = ICH >> (st + 1)
                sbm = (1 << st) * 128
                svv = src[:, :].rearrange("p (ns two sbm) -> p ns two sbm",
                                          ns=ns, two=2, sbm=sbm)
                dvv = dstb[:, :].rearrange("p (ns two sbm) -> p ns two sbm",
                                           ns=ns, two=2, sbm=sbm)
                if sbm >= 2048:
                    cut = int(sbm * FR / 8192.0) // 2 * 2
                    dv = (svv[:, :, :, 0:cut], dvv[:, :, :, 0:cut])
                    gv = (svv[:, :, :, cut:], dvv[:, :, :, cut:])
                else:
                    nscut = max(min(int(round(ns * FR / 8192.0)), ns - 1), 1)
                    dv = (svv[:, 0:nscut], dvv[:, 0:nscut])
                    gv = (svv[:, nscut:], dvv[:, nscut:])
                wv = [(sc, evac_done)] if st == 0 else [(gp, gstage)]
                wg = [(sc, evac_done)] if st == 0 else [(vc, vstage)]
                if st == 0 and mp >= 2:
                    wv.append((pso[par], pso[par].n))
                    wg.append((pso[par], pso[par].n))
                S.emit("vector", lambda e, dv=dv: e.tensor_tensor(
                    dv[1][:, :, 0, :], dv[0][:, :, 0, :], dv[0][:, :, 1, :],
                    ALU.add), waits=wv)
                S.emit("vector", lambda e, dv=dv: e.tensor_tensor(
                    dv[1][:, :, 1, :], dv[0][:, :, 0, :], dv[0][:, :, 1, :],
                    ALU.subtract), inc=vc)
                vstage = vc.n
                S.emit("gpsimd", lambda e, gv=gv: e.tensor_tensor(
                    gv[1][:, :, 0, :], gv[0][:, :, 0, :], gv[0][:, :, 1, :],
                    ALU.add), waits=wg)
                S.emit("gpsimd", lambda e, gv=gv: e.tensor_tensor(
                    gv[1][:, :, 1, :], gv[0][:, :, 0, :], gv[0][:, :, 1, :],
                    ALU.subtract), inc=gp)
                gstage = gp.n
                src = dstb
            bf_v[mp] = (vstage, gstage)
            S.emit("sync",
                   lambda e, mp=mp, par=par: e.dma_start(P.wdck[mp],
                                                         pnlo[par][:, :]),
                   waits=[(vc, vstage), (gp, gstage)], inc=pso[par], inc_n=16)
        S.emit("sync", None, waits=[(pso[0], pso[0].n), (pso[1], pso[1].n),
                                    (vc, vc.n), (gp, gp.n), (sc, sc.n)])

        with nc.Block("prep") as blk:
            S.run_block(blk)


def emit_ln(P, S, sems, bufs, src, col0, su, dest_chunk_fn,
            xld_waits=(), pe_waits=(), xbf_waits=()):
    ld, mm, sc, vc = sems["ld"], sems["mm"], sems["sc"], sems["vc"]
    xt, xbf = bufs["xt"], bufs["xbf"]
    mrow, vrow = bufs["mrow"], bufs["vrow"]
    psS, psM, psR = bufs["psS"], bufs["psM"], bufs["psR"]

    srcv = src[:, col0:col0 + T].rearrange("(kc p) t -> p kc t", p=128)
    S.emit("sync", lambda e: e.dma_start(_chunked(xt[:, 0:NCH * T], T), srcv),
           waits=xld_waits, inc=ld, inc_n=16)
    S.emit("scalar", lambda e: e.activation(xbf[:, 0:NCH * T], xt[:, 0:NCH * T],
                                            AF.Copy),
           waits=[(ld, ld.n)] + list(xbf_waits), inc=sc)

    def s1(e):
        ins = None
        for c in range(NCH):
            ins = e.matmul(psS[0:1, 0:T], P.onk[:, :],
                           xbf[:, c * T:(c + 1) * T],
                           start=(c == 0), stop=(c == NCH - 1))
        return ins
    S.emit("tensor", s1, waits=[(sc, sc.n)] + list(pe_waits), inc=mm)

    S.emit("scalar", lambda e: e.activation(mrow[0:1, 0:T], psS[0:1, 0:T],
                                            AF.Copy, scale=1.0 / DM),
           waits=[(mm, mm.n)], inc=sc)
    S.emit("tensor", lambda e: e.matmul(psM[:, 0:T], P.onr[0:1, :],
                                        mrow[0:1, 0:T], start=True, stop=True),
           waits=[(sc, sc.n)], inc=mm)

    def xm(e):
        ins = None
        for c in range(NCH):
            sl = slice(c * T, (c + 1) * T)
            ins = e.tensor_tensor(xt[:, sl], xt[:, sl], psM[:, 0:T],
                                  ALU.subtract)
        return ins
    S.emit("vector", xm, waits=[(mm, mm.n)], inc=vc)

    S.emit("scalar", lambda e: e.activation(xbf[:, 0:NCH * T], xt[:, 0:NCH * T],
                                            AF.Square),
           waits=[(vc, vc.n)], inc=sc)
    S.emit("tensor", s1, waits=[(sc, sc.n)], inc=mm)

    def vr(e):
        e.activation(vrow[0:1, 0:T], psS[0:1, 0:T], AF.Copy, scale=1.0 / DM)
        e.activation(vrow[0:1, 0:T], vrow[0:1, 0:T], AF.Ln, bias=P.fc_sb[0:1, 0:1])
        return e.activation(vrow[0:1, 0:T], vrow[0:1, 0:T], AF.Exp, scale=-0.5)
    S.emit("scalar", vr, waits=[(mm, mm.n)], inc=sc)

    S.emit("tensor", lambda e: e.matmul(psR[:, 0:T], P.onr[0:1, :],
                                        vrow[0:1, 0:T], start=True, stop=True),
           waits=[(sc, sc.n)], inc=mm)

    def stt(e):
        ins = None
        for c in range(NCH):
            scal = P.su_sb[:, c:c + 1] if su else 1.0
            ins = e.scalar_tensor_tensor(
                dest_chunk_fn(c), xt[:, c * T:(c + 1) * T], scal,
                psR[:, 0:T], ALU.mult, ALU.mult)
        return ins
    S.emit("vector", stt, waits=[(mm, mm.n)], inc=vc)
    return vc.n


def build_ffn_pass(P, src, dst, col0, ncols, tag, dst_col_off=0):
    nc = P.nc
    S = Sched(nc)
    ntiles = ncols // T
    TH = T // 2

    with ExitStack() as ctx:
        E = ctx.enter_context
        ld = Sem(ctx, nc, tag + "_ld")
        mm = Sem(ctx, nc, tag + "_mm")
        sc = Sem(ctx, nc, tag + "_sc")
        vc = Sem(ctx, nc, tag + "_vc")
        wld = [Sem(ctx, nc, tag + f"_wld{i}") for i in range(2)]
        dld = [Sem(ctx, nc, tag + f"_dld{i}") for i in range(2)]
        xml = [Sem(ctx, nc, tag + f"_xml{i}") for i in range(2)]
        stb = [Sem(ctx, nc, tag + f"_stb{i}") for i in range(2)]

        raw = E(nc.sbuf_tensor(tag + "_raw", [128, 20480], BF16))
        bfA = carve(raw, 0, 128, NCH * TH, F32)
        xbf = carve(raw, 0, 128, NCH * T, BF16)
        bfB = carve(raw, 16384, 128, NCH * TH, F32)
        y16 = carve(raw, 32768, 128, NCH * TH, BF16)

        hbuf = E(nc.sbuf_tensor(tag + "_h", [128, ICH * T], BF16))
        xt = E(nc.sbuf_tensor(tag + "_xt", [128, NCH * T], F32))
        yh = E(nc.sbuf_tensor(tag + "_yh", [128, NCH * T], BF16))
        mrow = E(nc.sbuf_tensor(tag + "_mrow", [1, T], F32))
        vrow = E(nc.sbuf_tensor(tag + "_vrow", [1, T], F32))
        gscr = [E(nc.sbuf_tensor(tag + f"_g{i}", [128, T], F32))
                for i in range(2)]
        pug = [E(nc.sbuf_tensor(tag + f"_pug{i}", [128, 2 * DM], BF16))
               for i in range(2)]
        dpnh = [E(nc.sbuf_tensor(tag + f"_dp{i}", [128, INTER // 2], BF16))
                for i in range(2)]
        xm = [E(nc.sbuf_tensor(tag + f"_xm{i}", [128, T], F32)) for i in range(2)]
        ob = [E(nc.sbuf_tensor(tag + f"_ob{i}", [128, T], F32)) for i in range(2)]
        psB = [E(nc.psum_tensor(tag + f"_ps{i}", [128, 512], F32))
               for i in range(8)]
        psS, psM, psR = psB[6], psB[7], psB[6]
        psH = [psB[0], psB[1]]

        stt_ln = {}
        bfly = {}
        h128e = {}
        stt_last = {}
        silu_cnt = {}
        fillA = {}
        grpB = {}
        pb_stt = {}
        xmsub = {}

        def ln_steps(t):
            tc0 = col0 + t * T
            steps = []
            srcv = src[:, tc0:tc0 + T].rearrange("(kc p) t -> p kc t", p=128)

            def s_load():
                w = [(vc, stt_ln.get(t - 1, 0))]
                S.emit("sync", lambda e: e.dma_start(
                    _chunked(xt[:, :], T), srcv), waits=w, inc=ld, inc_n=16)
                xl = ld.n
                w2 = [(ld, xl), (vc, bfly.get((t - 1, 1), 0))]
                if t >= 2:
                    w2.append((mm, grpB[(t - 2, NCH - 1, 1)]))
                S.emit("scalar", lambda e: e.activation(
                    xbf[:, :], xt[:, :], AF.Copy), waits=w2, inc=sc)
            steps.append(s_load)

            def mk_s1():
                def f(e):
                    ins = None
                    for c in range(NCH):
                        ins = e.matmul(psS[0:1, 0:T], P.onk[:, :],
                                       xbf[:, c * T:(c + 1) * T],
                                       start=(c == 0), stop=(c == NCH - 1))
                    return ins
                return f

            def s_s1():
                w = [(sc, sc.n), (vc, stt_ln.get(t - 1, 0))]
                S.emit("tensor", mk_s1(), waits=w, inc=mm)
                mr_mm = mm.n
                S.emit("scalar", lambda e: e.activation(
                    mrow[0:1, 0:T], psS[0:1, 0:T], AF.Copy, scale=1.0 / DM),
                    waits=[(mm, mr_mm)], inc=sc)
            steps.append(s_s1)

            def s_mb():
                w = [(sc, sc.n), (vc, xmsub.get(t - 1, 0))]
                S.emit("tensor", lambda e: e.matmul(
                    psM[:, 0:T], P.onr[0:1, :], mrow[0:1, 0:T],
                    start=True, stop=True), waits=w, inc=mm)
                mb_mm = mm.n

                def f(e):
                    ins = None
                    for c in range(NCH):
                        sl = slice(c * T, (c + 1) * T)
                        ins = e.tensor_tensor(xt[:, sl], xt[:, sl],
                                              psM[:, 0:T], ALU.subtract)
                    return ins
                S.emit("vector", f, waits=[(mm, mb_mm)], inc=vc)
                xmsub[t] = vc.n
                S.emit("scalar", lambda e: e.activation(
                    xbf[:, :], xt[:, :], AF.Square),
                    waits=[(vc, vc.n)], inc=sc)
            steps.append(s_mb)

            def s_s2():
                S.emit("tensor", mk_s1(), waits=[(sc, sc.n)], inc=mm)
                s2_mm = mm.n

                def f(e):
                    e.activation(vrow[0:1, 0:T], psS[0:1, 0:T], AF.Copy,
                                 scale=1.0 / DM)
                    e.activation(vrow[0:1, 0:T], vrow[0:1, 0:T], AF.Ln,
                                 bias=P.fc_sb[0:1, 0:1])
                    return e.activation(vrow[0:1, 0:T], vrow[0:1, 0:T],
                                        AF.Exp, scale=-0.5)
                S.emit("scalar", f, waits=[(mm, s2_mm)], inc=sc)
            steps.append(s_s2)

            def s_rb():
                S.emit("tensor", lambda e: e.matmul(
                    psR[:, 0:T], P.onr[0:1, :], vrow[0:1, 0:T],
                    start=True, stop=True), waits=[(sc, sc.n)], inc=mm)
            steps.append(s_rb)

            def mk_stt(half):
                def f():
                    def g(e):
                        ins = None
                        for c in range(NCH):
                            scal = P.su_sb[:, c:c + 1]
                            ins = e.scalar_tensor_tensor(
                                bfA[:, c * TH:(c + 1) * TH],
                                xt[:, c * T + half * TH:
                                   c * T + half * TH + TH],
                                scal, psR[:, half * TH:half * TH + TH],
                                ALU.mult, ALU.mult)
                        return ins
                    S.emit("vector", g, waits=[(mm, mm.n)], inc=vc)
                    if half == 1:
                        stt_ln[t] = vc.n
                return f

            def mk_bfly(half):
                def f():
                    srcb, dsts = bfA, [bfB, bfA, bfB, y16]
                    for s in range(4):
                        dstb = dsts[s]
                        ns = NCH >> (s + 1)
                        sbm = (1 << s) * TH
                        svv = srcb[:, 0:NCH * TH].rearrange(
                            "p (ns two sbm) -> p ns two sbm",
                            ns=ns, two=2, sbm=sbm)
                        dvv = dstb[:, 0:NCH * TH].rearrange(
                            "p (ns two sbm) -> p ns two sbm",
                            ns=ns, two=2, sbm=sbm)
                        S.emit("vector", lambda e, svv=svv, dvv=dvv:
                               e.tensor_tensor(dvv[:, :, 0, :], svv[:, :, 0, :],
                                               svv[:, :, 1, :], ALU.add))
                        S.emit("vector", lambda e, svv=svv, dvv=dvv:
                               e.tensor_tensor(dvv[:, :, 1, :], svv[:, :, 0, :],
                                               svv[:, :, 1, :], ALU.subtract),
                               inc=vc)
                        srcb = dstb
                    bfly[(t, half)] = vc.n
                return f

            def mk_h128(half, part):
                def f():
                    for c in range(8 * part, 8 * part + 8):
                        bank = c % 2
                        w = [(vc, bfly[(t, half)])]
                        idx = half * NCH + c
                        if (t, idx - 2) in h128e:
                            w.append((sc, h128e[(t, idx - 2)]))
                        elif t > 0 and idx < 2:
                            w.append((sc, h128e[(t - 1, 2 * NCH - 2 + idx)]))
                        if t > 0 and idx < 2:
                            w.append((vc, stt_last[(t - 1, ICH - 1)]))
                            w.append((sc, silu_cnt[(t - 1, ICH - 1)]))
                        S.emit("tensor",
                               lambda e, c=c, bank=bank: e.matmul(
                                   psH[bank][:, 0:TH], P.hH[:, :],
                                   y16[:, c * TH:(c + 1) * TH],
                                   start=True, stop=True),
                               waits=w, inc=mm)
                        S.emit("scalar",
                               lambda e, c=c, bank=bank, half=half:
                               e.activation(
                                   yh[:, c * T + half * TH:
                                      c * T + half * TH + TH],
                                   psH[bank][:, 0:TH], AF.Copy),
                               waits=[(mm, mm.n)], inc=sc)
                        h128e[(t, idx)] = sc.n
                return f

            steps.append(mk_stt(0))
            steps.append(mk_bfly(0))
            steps.append(mk_h128(0, 0))
            steps.append(mk_h128(0, 1))
            steps.append(mk_stt(1))
            steps.append(mk_bfly(1))
            steps.append(mk_h128(1, 0))
            steps.append(mk_h128(1, 1))
            return steps

        def phaseA(t, nxt_steps):
            inter = {0: 0, 6: 1, 10: 2, 14: 3, 18: 4, 40: 5}
            for ic in range(ICH):
                if ic in inter and nxt_steps:
                    nxt_steps[inter[ic]]()
                par = ic % 2
                w = []
                if (t, ic - 2) in fillA:
                    w = [(mm, fillA[(t, ic - 2)])]
                elif t > 0:
                    w = [(mm, fillA[(t - 1, ICH - 2 + ic)])]
                if t == 0 and src is P.xin:
                    w.append((P.c_ugc[ic // 8], 32))
                S.emit("sync", lambda e, ic=ic, par=par: e.dma_start(
                    pug[par][:, :], P.wugck[ic]),
                    waits=w, inc=wld[par], inc_n=16)

                uB, gB = psB[0 + 2 * par], psB[1 + 2 * par]
                w = [(wld[par], wld[par].n), (sc, h128e[(t, 2 * NCH - 1)])]
                if (t, ic - 2) in stt_last:
                    w.append((vc, stt_last[(t, ic - 2)]))
                else:
                    w.append((vc, bfly[(t, 1)]))

                def fillfn(e, par=par, uB=uB, gB=gB):
                    ins = None
                    for off, bk in ((0, uB), (DM, gB)):
                        for kc in range(NCH):
                            ins = e.matmul(bk[:, :],
                                           pug[par][:, off + kc * 128:
                                                    off + (kc + 1) * 128],
                                           yh[:, kc * T:(kc + 1) * T],
                                           start=(kc == 0),
                                           stop=(kc == NCH - 1))
                    return ins
                S.emit("tensor", fillfn, waits=w, inc=mm)
                fillA[(t, ic)] = mm.n

                gs = gscr[ic % 2]
                sw = [(mm, mm.n)]
                if (t, ic - 2) in stt_last:
                    sw.append((vc, stt_last[(t, ic - 2)]))
                S.emit("scalar", lambda e, ic=ic, gB=gB, gs=gs: e.activation(
                    gs[:, :], gB[:, :], AF.Silu,
                    scale=P.sg_sb[:, ic:ic + 1]),
                    waits=sw, inc=sc)
                silu_cnt[(t, ic)] = sc.n

                S.emit("vector", lambda e, ic=ic, uB=uB, gs=gs:
                       e.scalar_tensor_tensor(
                           hbuf[:, ic * T:(ic + 1) * T], uB[:, :],
                           P.al_sb[:, ic:ic + 1], gs[:, :], ALU.mult, ALU.mult),
                       waits=[(sc, sc.n)], inc=vc)
                stt_last[(t, ic)] = vc.n

        def phaseB(t, nxt_steps):
            tc0 = col0 + t * T
            inter = {1: 6, 2: 7, 3: 8, 5: 9, 7: 10, 9: 11, 11: 12}
            for m in range(NCH):
                if m in inter and nxt_steps:
                    nxt_steps[inter[m]]()
                par = m % 2
                dB = psB[4 + par]
                for hf in range(2):
                    w = []
                    if (t, m - 1, hf) in grpB:
                        w = [(mm, grpB[(t, m - 1, hf)])]
                    elif t > 0:
                        w = [(mm, grpB[(t - 1, NCH - 1, hf)])]
                    S.emit("scalar", lambda e, m=m, hf=hf: e.dma_start(
                        dpnh[hf][:, :],
                        P.wdck[m][:, hf * 4096:(hf + 1) * 4096]),
                        waits=w, inc=dld[hf], inc_n=16)
                w2 = []
                if (t, m - 2) in pb_stt:
                    w2 = [(vc, pb_stt[(t, m - 2)])]
                elif t > 0:
                    w2 = [(vc, pb_stt[(t - 1, NCH - 2 + m)])]
                S.emit("sync", lambda e, m=m, par=par, tc0=tc0: e.dma_start(
                    xm[par][:, :], src[m * 128:(m + 1) * 128, tc0:tc0 + T]),
                    waits=w2, inc=xml[par], inc_n=16)

                for hf in range(2):
                    w = [(dld[hf], dld[hf].n)]
                    if hf == 0:
                        if (t, m - 2) in pb_stt:
                            w.append((vc, pb_stt[(t, m - 2)]))
                        elif t > 0:
                            w.append((vc, pb_stt[(t - 1, NCH - 2 + m)]))
                        if m == 0:
                            w.append((vc, stt_last[(t, ICH - 1)]))

                    def dfill(e, par=par, hf=hf, dB=dB):
                        ins = None
                        for j in range(32):
                            kc = hf * 32 + j
                            ins = e.matmul(
                                dB[:, :], dpnh[hf][:, j * 128:(j + 1) * 128],
                                hbuf[:, kc * T:(kc + 1) * T],
                                start=(kc == 0), stop=(kc == ICH - 1))
                        return ins
                    S.emit("tensor", dfill, waits=w, inc=mm)
                    grpB[(t, m, hf)] = mm.n

                w = [(mm, mm.n), (xml[par], xml[par].n), (stb[par], stb[par].n)]
                S.emit("vector", lambda e, m=m, par=par, dB=dB:
                       e.scalar_tensor_tensor(
                           ob[par][:, :], dB[:, :], P.bdp_sb[:, m:m + 1],
                           xm[par][:, :], ALU.mult, ALU.add),
                       waits=w, inc=vc)
                pb_stt[(t, m)] = vc.n

                S.emit("sync", lambda e, m=m, par=par, tc0=tc0: e.dma_start(
                    dst[m * 128:(m + 1) * 128,
                        tc0 + dst_col_off: tc0 + dst_col_off + T],
                    ob[par][:, :]),
                    waits=[(vc, vc.n)], inc=stb[par], inc_n=16)

        st0 = ln_steps(0)
        for f in st0:
            f()
        for t in range(ntiles):
            nxt = ln_steps(t + 1) if t + 1 < ntiles else None
            phaseA(t, nxt)
            phaseB(t, nxt)

        S.emit("sync", None, waits=[(stb[0], stb[0].n), (stb[1], stb[1].n)])
        with nc.Block(tag) as blk:
            S.run_block(blk)

def build_attn(P, a1_only=False, serial_ln=False, no_maskmm=False,
               no_psumexp=False, no_p2=False, no_wo=False, s_only=False,
               no_p1c=False, s_extra=0):
    nc = P.nc
    S = Sched(nc)
    with ExitStack() as ctx:
        E = ctx.enter_context
        ld = Sem(ctx, nc, "a_ld")
        mm = Sem(ctx, nc, "a_mm")
        sc = Sem(ctx, nc, "a_sc")
        vc = Sem(ctx, nc, "a_vc")
        wld = [Sem(ctx, nc, f"a_wld{i}") for i in range(2)]
        xml = [Sem(ctx, nc, f"a_xml{i}") for i in range(2)]
        stb = [Sem(ctx, nc, f"a_stb{i}") for i in range(2)]
        csl = Sem(ctx, nc, "a_csl")

        raw = E(nc.sbuf_tensor("a_raw", [128, 33280], BF16))
        xt = carve(raw, 0, 128, NCH * T, F32)
        xbf = carve(raw, 32768, 128, NCH * T, BF16)
        cs = {}
        for i, nm in enumerate(("cq", "sq", "ck", "sk")):
            cs[nm] = carve(raw, 49152 + i * 2048, 128, T, F32)
        rsc = carve(raw, 57344, 128, T, F32)
        rsc2 = carve(raw, 59392, 128, T, F32)
        vtmp = carve(raw, 61440, 128, T, BF16)
        mrow = carve(raw, 62464, 1, T, F32)
        vrow = carve(raw, 64512, 1, T, F32)
        escr = [carve(raw, 0, 128, 640, F32), carve(raw, 2560, 128, 640, F32)]
        pb = [carve(raw, 5120, 128, 640, BF16), carve(raw, 6400, 128, 640, BF16)]
        pt = [carve(raw, 7680, 128, 640, BF16), carve(raw, 8960, 128, 640, BF16)]
        atile = carve(raw, 10240, 128, NH * T, BF16)
        xm = [carve(raw, 26624, 128, T, F32), carve(raw, 28672, 128, T, F32)]
        ob = [carve(raw, 30720, 128, T, F32), carve(raw, 32768, 128, T, F32)]

        ya = E(nc.sbuf_tensor("a_ya", [128, NCH * T], BF16))
        qbuf = E(nc.sbuf_tensor("a_q", [128, NH * TOWN], BF16))
        kbuf = E(nc.sbuf_tensor("a_k", [128, NKV * TLOC], BF16))
        vtb = E(nc.sbuf_tensor("a_vt", [128, NKV * TLOC], BF16))
        wpan = [E(nc.sbuf_tensor(f"a_wp{i}", [128, DM], BF16)) for i in range(2)]
        krw = E(nc.sbuf_tensor("a_krw", [1, TLOC], BF16))
        bmk = E(nc.sbuf_tensor("a_bmk", [128, 640], BF16))
        nmx = [E(nc.sbuf_tensor(f"a_nm{i}", [128, 1], F32)) for i in range(2)]
        ssm = [E(nc.sbuf_tensor(f"a_ss{i}", [128, 1], F32)) for i in range(2)]
        rcb = [E(nc.sbuf_tensor(f"a_rc{i}", [128, 1], F32)) for i in range(2)]

        psSab = [E(nc.psum_tensor(f"a_psS{i}", [128, 640], F32)) for i in range(2)]
        psO = [E(nc.psum_tensor(f"a_psO{i}", [128, 512], F32)) for i in range(2)]
        psT = E(nc.psum_tensor("a_psT", [128, 640], BF16))
        psX = E(nc.psum_tensor("a_psX", [128, 512], F32))
        psM = psSab[0][:, 0:512]

        S.emit("sync", lambda e: e.dma_start(krw[:, :], P.krow), inc=ld, inc_n=16)
        S.emit("sync", lambda e: e.dma_start(bmk[:, :], P.bmsk), inc=ld, inc_n=16)
        base_ld = ld.n

        stt_ln = {}
        xmsub = {}
        qkv_fill = {}
        slot_cons = {}
        rope_done = {}
        nocs = {}

        def ln_steps(t):
            tc0 = t * T
            srcv = P.xcur[:, tc0:tc0 + T].rearrange("(kc p) t -> p kc t", p=128)
            steps = []

            def s_load():
                w = [(vc, stt_ln.get(t - 1, 0))] if t > 0 else [(ld, base_ld)]
                S.emit("sync", lambda e: e.dma_start(
                    _chunked(xt[:, :], T), srcv), waits=w, inc=ld, inc_n=16)
                xl = ld.n
                S.emit("scalar", lambda e: e.activation(
                    xbf[:, :], xt[:, :], AF.Copy), waits=[(ld, xl)], inc=sc)
            steps.append(s_load)

            def mk_s1():
                def f(e):
                    ins = None
                    for c in range(NCH):
                        ins = e.matmul(psX[0:1, 0:T], P.onk[:, :],
                                       xbf[:, c * T:(c + 1) * T],
                                       start=(c == 0), stop=(c == NCH - 1))
                    return ins
                return f

            def s_s1():
                S.emit("tensor", mk_s1(),
                       waits=[(sc, sc.n), (vc, stt_ln.get(t - 1, 0))], inc=mm)
                S.emit("scalar", lambda e: e.activation(
                    mrow[0:1, 0:T], psX[0:1, 0:T], AF.Copy, scale=1.0 / DM),
                    waits=[(mm, mm.n)], inc=sc)
            steps.append(s_s1)

            def s_mb():
                w = [(sc, sc.n), (vc, xmsub.get(t - 1, 0))]
                S.emit("tensor", lambda e: e.matmul(
                    psM, P.onr[0:1, :], mrow[0:1, 0:T],
                    start=True, stop=True), waits=w, inc=mm)
                mb_mm = mm.n

                def f(e):
                    ins = None
                    for c in range(NCH):
                        sl = slice(c * T, (c + 1) * T)
                        ins = e.tensor_tensor(xt[:, sl], xt[:, sl], psM,
                                              ALU.subtract)
                    return ins
                S.emit("vector", f, waits=[(mm, mb_mm)], inc=vc)
                xmsub[t] = vc.n
                S.emit("scalar", lambda e: e.activation(
                    xbf[:, :], xt[:, :], AF.Square),
                    waits=[(vc, vc.n)], inc=sc)
            steps.append(s_mb)

            def s_s2():
                S.emit("tensor", mk_s1(), waits=[(sc, sc.n)], inc=mm)
                s2_mm = mm.n

                def f(e):
                    e.activation(vrow[0:1, 0:T], psX[0:1, 0:T], AF.Copy,
                                 scale=1.0 / DM)
                    e.activation(vrow[0:1, 0:T], vrow[0:1, 0:T], AF.Ln,
                                 bias=P.fc_sb[0:1, 0:1])
                    return e.activation(vrow[0:1, 0:T], vrow[0:1, 0:T],
                                        AF.Exp, scale=-0.5)
                S.emit("scalar", f, waits=[(mm, s2_mm)], inc=sc)
            steps.append(s_s2)

            def s_rb():
                S.emit("tensor", lambda e: e.matmul(
                    psX[:, 0:T], P.onr[0:1, :], vrow[0:1, 0:T],
                    start=True, stop=True), waits=[(sc, sc.n)], inc=mm)
            steps.append(s_rb)

            def s_stt():
                w = [(mm, mm.n)]
                if t > 0:
                    w.append((mm, qkv_fill[(t - 1, nocs[t - 1] - 1)]))

                def f(e):
                    ins = None
                    for c in range(NCH):
                        ins = e.scalar_tensor_tensor(
                            ya[:, c * T:(c + 1) * T], xt[:, c * T:(c + 1) * T],
                            1.0, psX[:, 0:T], ALU.mult, ALU.mult)
                    return ins
                S.emit("vector", f, waits=w, inc=vc)
                stt_ln[t] = vc.n
            steps.append(s_stt)

            def s_cs():
                for nm, srcap in (("cq", P.cosq), ("sq", P.sinq),
                                  ("ck", P.cosk), ("sk", P.sink)):
                    w = [(vc, rope_done.get(t - 1, 0))]
                    S.emit("sync", lambda e, nm=nm, srcap=srcap: e.dma_start(
                        cs[nm][:, :], srcap[:, tc0:tc0 + T]), waits=w,
                        inc=csl, inc_n=16)
            steps.append(s_cs)
            return steps

        for t in range(5):
            tc0 = t * T
            has_q = tc0 + T > HALO
            ocs = []
            if has_q:
                for h in range(NH):
                    ocs.append(("q", h, P.wqck))
            for h in range(NKV):
                ocs.append(("k", h, P.wkck))
            for h in range(NKV):
                ocs.append(("v", h, P.wvck))
            nocs[t] = len(ocs)

            if t == 0:
                for f in ln_steps(0):
                    f()
            if serial_ln and t > 0:
                for f in ln_steps(t):
                    f()
            nxt = None if serial_ln else (ln_steps(t + 1) if t < 4 else None)
            n_oc = len(ocs)
            inter = {}
            if nxt:
                step_pos = [1, 4, 7, 10, 13] if n_oc >= 15 else [1, 2, 3, 4, 5]
                for j, p_ in enumerate(step_pos):
                    inter[p_] = j

            qkvB = [psO[0][:, :], psO[1][:, :], psSab[1][:, 0:512]]
            for oi, (kind, h, wsrc) in enumerate(ocs):
                if nxt and oi in inter:
                    nxt[inter[oi]]()
                par = oi % 2
                bank = qkvB[oi % 3]
                w = []
                if (t, oi - 2) in qkv_fill:
                    w = [(mm, qkv_fill[(t, oi - 2)])]
                elif t > 0:
                    w = [(mm, qkv_fill[(t - 1, nocs[t - 1] - 2 + oi)])]
                else:
                    w = [(P.c_at, P.c_at.n)]
                S.emit("scalar", lambda e, h=h, wsrc=wsrc, par=par: e.dma_start(
                    wpan[par][:, :], wsrc[h]),
                    waits=w, inc=wld[par], inc_n=16)

                w = [(wld[par], wld[par].n), (vc, stt_ln[t])]
                pc = slot_cons.get((t, oi - 3))
                if pc is None and t > 0:
                    pc = slot_cons.get((t - 1, nocs[t - 1] - 3 + oi))
                if pc is not None:
                    w.append((pc[0], pc[1]))

                def qf(e, par=par, bank=bank):
                    ins = None
                    for kc in range(NCH):
                        ins = e.matmul(bank[:, :],
                                       wpan[par][:, kc * 128:(kc + 1) * 128],
                                       ya[:, kc * T:(kc + 1) * T],
                                       start=(kc == 0), stop=(kc == NCH - 1))
                    return ins
                S.emit("tensor", qf, waits=w, inc=mm)
                qkv_fill[(t, oi)] = mm.n

                if kind in ("q", "k"):
                    co, so = (cs["cq"], cs["sq"]) if kind == "q" else                              (cs["ck"], cs["sk"])
                    if kind == "q":
                        dest = qbuf[:, h * TOWN + (tc0 - HALO):
                                    h * TOWN + (tc0 - HALO) + T]
                    else:
                        dest = kbuf[:, h * TLOC + tc0: h * TLOC + tc0 + T]

                    def rope(e, bank=bank, co=co, so=so, dest=dest):
                        e.tensor_tensor(rsc[:, :], bank[:, :], co[:, :],
                                        ALU.mult)
                        e.tensor_tensor(rsc2[0:64, :], bank[64:128, :],
                                        so[0:64, :], ALU.mult)
                        e.tensor_tensor(rsc2[64:128, :], bank[0:64, :],
                                        so[64:128, :], ALU.mult)
                        return e.tensor_tensor(dest, rsc[:, :], rsc2[:, :],
                                               ALU.add)
                    S.emit("vector", rope, waits=[(mm, mm.n), (csl, csl.n)],
                           inc=vc)
                    rope_done[t] = vc.n
                    slot_cons[(t, oi)] = (vc, vc.n)
                else:
                    S.emit("scalar", lambda e, bank=bank: e.activation(
                        vtmp[:, :], bank[:, :], AF.Copy),
                        waits=[(mm, mm.n)], inc=sc)
                    slot_cons[(t, oi)] = (sc, sc.n)
                    for b in range(4):
                        S.emit("tensor", lambda e, b=b: e.transpose(
                            psT[:, 0:128], vtmp[:, b * 128:(b + 1) * 128],
                            P.idn[:, :]),
                            waits=[(sc, sc.n)], inc=mm)
                        blk_i = tc0 // 128 + b
                        S.emit("scalar",
                               lambda e, h=h, blk_i=blk_i: e.activation(
                                   vtb[:, (h * 20 + blk_i) * 128:
                                       (h * 20 + blk_i) * 128 + 128],
                                   psT[:, 0:128], AF.Copy),
                               waits=[(mm, mm.n)], inc=sc)
            if nxt:
                nxt[5]()
                nxt[6]()
        a1_vc = vc.n
        a1_sc = sc.n
        a1_mm = mm.n

        if a1_only:
            S.emit("sync", None, waits=[(vc, a1_vc), (sc, a1_sc), (mm, a1_mm),
                                        (csl, csl.n)])
            with nc.Block("attn") as blk:
                S.run_block(blk)
            return

        NIT = 4 * NH
        sfill = {}
        nmx_v = {}
        exp_s = {}
        sum_v = {}
        psc_v = {}
        tr_m = {}
        pte_s = {}
        pv_m = {}
        oev_s = {}
        wo_fill = {}
        wo_stt = {}

        for wt in range(4):
            t0own = wt * T
            iters = [(t0own // 128 + qb4, h) for qb4 in range(4)
                     for h in range(NH)]

            def kprev(d, it, wt=wt):
                if it >= 0:
                    return d.get((wt, it))
                if wt > 0:
                    return d.get((wt - 1, NIT + it))
                return None

            def emit_S(it, wt=wt, iters=iters):
                i, h = iters[it]
                par = it % 2
                sAB = psSab[par]
                kv = h // NKV
                qsl = qbuf[:, h * TOWN + i * 128: h * TOWN + i * 128 + 128]
                w = []
                if s_only:
                    pv = kprev(nmx_v, it - 2)
                    if pv is not None:
                        w.append((vc, pv))
                    else:
                        w.append((vc, a1_vc))
                        w.append((sc, a1_sc))
                else:
                    pv = kprev(exp_s, it - 2)
                    if pv is not None:
                        w.append((sc, pv))
                    else:
                        w.append((vc, a1_vc))
                        w.append((sc, a1_sc))
                if no_p1c:
                    pv = kprev(sum_v, it - 2)
                    if pv is not None:
                        w.append((vc, pv))

                def f(e, i=i, kv=kv, sAB=sAB, qsl=qsl):
                    e.matmul(sAB[:, 0:512], P.onb[0:1, :],
                             krw[0:1, i * 128: i * 128 + 512],
                             start=True, stop=False)
                    if not no_maskmm:
                        e.matmul(sAB[:, 0:512], P.idn[:, :], bmk[:, 0:512],
                                 start=False, stop=False)
                    e.matmul(sAB[:, 0:512], qsl,
                             kbuf[:, kv * TLOC + i * 128:
                                  kv * TLOC + i * 128 + 512],
                             start=False, stop=True)
                    if not no_maskmm:
                        e.matmul(sAB[:, 512:640], P.idn[:, :], bmk[:, 512:640],
                                 start=True, stop=False)
                        return e.matmul(sAB[:, 512:640], qsl,
                                        kbuf[:, kv * TLOC + i * 128 + 512:
                                             kv * TLOC + i * 128 + 640],
                                        start=False, stop=True)
                    return e.matmul(sAB[:, 512:640], qsl,
                                    kbuf[:, kv * TLOC + i * 128 + 512:
                                         kv * TLOC + i * 128 + 640],
                                    start=True, stop=True)
                S.emit("tensor", f, waits=w, inc=mm)
                sfill[(wt, it)] = mm.n

            def p1a(it, wt=wt):
                par = it % 2
                w = [(mm, sfill[(wt, it)])]
                pv = kprev(exp_s, it - 2)
                if pv is not None:
                    w.append((sc, pv))
                if s_only:
                    S.emit("vector", lambda e, par=par: e.tensor_copy(
                        escr[par][:, :], psSab[par][:, :]), waits=w, inc=vc)
                    if s_extra == 1:
                        S.emit("vector", lambda e, par=par: e.tensor_reduce(
                            nmx[par][:, 0:1], psSab[par][:, :], AX.X, ALU.max,
                            negate=True), inc=vc)
                    elif s_extra == 2:
                        S.emit("vector", lambda e, par=par: e.tensor_reduce(
                            nmx[par][:, 0:1], escr[par][:, :], AX.X, ALU.max,
                            negate=True), inc=vc)
                    elif s_extra == 3:
                        S.emit("scalar", lambda e, par=par: e.activation(
                            pb[par][:, :], escr[par][:, :], AF.Exp), inc=sc)
                    elif s_extra == 6:
                        S.emit("vector", lambda e, par=par: e.tensor_reduce(
                            nmx[par][:, 0:1], escr[par][:, :], AX.X, ALU.max,
                            negate=True), inc=vc)
                        S.emit("scalar", lambda e, par=par: e.activation(
                            pb[par][:, :], escr[par][:, :], AF.Exp,
                            bias=nmx[par][:, 0:1]),
                            waits=[(vc, vc.n)], inc=sc)
                    elif s_extra == 7:
                        S.emit("vector", lambda e, par=par: e.tensor_reduce(
                            nmx[par][:, 0:1], escr[par][:, :], AX.X, ALU.max,
                            negate=True), inc=vc)
                        S.emit("scalar", lambda e, par=par: e.activation(
                            pb[par][:, :], escr[par][:, :], AF.Exp,
                            bias=nmx[par][:, 0:1]),
                            waits=[(vc, vc.n)], inc=sc)
                        S.emit("vector", lambda e, par=par: e.tensor_reduce(
                            ssm[par][:, 0:1], pb[par][:, :], AX.X, ALU.add),
                            waits=[(sc, sc.n)], inc=vc)
                    elif s_extra == 4:
                        S.emit("vector", lambda e, par=par: e.tensor_reduce(
                            nmx[par][:, 0:1], psSab[par][:, :], AX.X, ALU.max,
                            negate=True), inc=vc)
                        S.emit("scalar", lambda e, par=par: e.activation(
                            escr[par][:, :], psSab[par][:, :], AF.Exp,
                            bias=nmx[par][:, 0:1]),
                            waits=[(vc, vc.n)], inc=sc)
                    nmx_v[(wt, it)] = vc.n
                    return
                if no_psumexp:
                    pv2 = kprev(psc_v, it - 2)
                    if pv2 is not None:
                        w.append((vc, pv2))
                    S.emit("vector", lambda e, par=par: e.tensor_copy(
                        escr[par][:, :], psSab[par][:, :]), waits=w, inc=vc)
                    S.emit("vector", lambda e, par=par: e.tensor_reduce(
                        nmx[par][:, 0:1], escr[par][:, :], AX.X, ALU.max,
                        negate=True), inc=vc)
                else:
                    S.emit("vector", lambda e, par=par: e.tensor_reduce(
                        nmx[par][:, 0:1], psSab[par][:, :], AX.X, ALU.max,
                        negate=True),
                        waits=w, inc=vc)
                nmx_v[(wt, it)] = vc.n

            def p1b(it, wt=wt):
                par = it % 2
                w = [(vc, nmx_v[(wt, it)])]
                pv = kprev(psc_v, it - 2)
                if pv is not None:
                    w.append((vc, pv))
                if no_psumexp:
                    S.emit("scalar", lambda e, par=par: e.activation(
                        escr[par][:, :], escr[par][:, :], AF.Exp,
                        bias=nmx[par][:, 0:1]),
                        waits=w, inc=sc)
                else:
                    S.emit("scalar", lambda e, par=par: e.activation(
                        escr[par][:, :], psSab[par][:, :], AF.Exp,
                        bias=nmx[par][:, 0:1]),
                        waits=w, inc=sc)
                exp_s[(wt, it)] = sc.n
                S.emit("vector", lambda e, par=par: e.tensor_reduce(
                    ssm[par][:, 0:1], escr[par][:, :], AX.X, ALU.add),
                    waits=[(sc, sc.n)], inc=vc)
                sum_v[(wt, it)] = vc.n

            def p1c(it, wt=wt):
                par = it % 2

                def rcp(e, par=par):
                    e.activation(rcb[par][:, 0:1], ssm[par][:, 0:1], AF.Ln)
                    return e.activation(rcb[par][:, 0:1], rcb[par][:, 0:1],
                                        AF.Exp, scale=-1.0)
                S.emit("scalar", rcp, waits=[(vc, sum_v[(wt, it)])], inc=sc)
                w = [(sc, sc.n)]
                pv = kprev(tr_m, it - 2)
                if pv is not None:
                    w.append((mm, pv))
                S.emit("vector", lambda e, par=par: e.tensor_scalar_mul(
                    pb[par][:, :], escr[par][:, :], rcb[par][:, 0:1]),
                    waits=w, inc=vc)
                psc_v[(wt, it)] = vc.n

            def p2(it, wt=wt, iters=iters):
                i, h = iters[it]
                par = it % 2
                kv = h // NKV
                qb4 = i - (wt * 4)
                w = [(vc, psc_v[(wt, it)])]
                pv = kprev(pte_s, it - 1)
                if pv is not None:
                    w.append((sc, pv))

                def tr(e, par=par):
                    ins = None
                    for j in range(5):
                        ins = e.transpose(psT[:, j * 128:(j + 1) * 128],
                                          pb[par][:, j * 128:(j + 1) * 128],
                                          P.idn[:, :])
                    return ins
                S.emit("tensor", tr, waits=w, inc=mm)
                tr_m[(wt, it)] = mm.n
                w = [(mm, mm.n)]
                pv = kprev(pv_m, it - 2)
                if pv is not None:
                    w.append((mm, pv))
                S.emit("scalar", lambda e, par=par: e.activation(
                    pt[par][:, :], psT[:, :], AF.Copy),
                    waits=w, inc=sc)
                pte_s[(wt, it)] = sc.n

                w = [(sc, sc.n)]
                if it >= 2:
                    w.append((sc, oev_s[(wt, it - 2)]))
                elif wt > 0:
                    w.append((vc, wo_stt[(wt - 1, NCH - 2 + it)]))

                def pvf(e, i=i, kv=kv, par=par):
                    ins = None
                    for j in range(5):
                        blk_i = i + j
                        ins = e.matmul(psO[par][:, 0:128],
                                       vtb[:, (kv * 20 + blk_i) * 128:
                                           (kv * 20 + blk_i) * 128 + 128],
                                       pt[par][:, j * 128:(j + 1) * 128],
                                       start=(j == 0), stop=(j == 4))
                    return ins
                S.emit("tensor", pvf, waits=w, inc=mm)
                pv_m[(wt, it)] = mm.n
                S.emit("scalar", lambda e, h=h, qb4=qb4, par=par: e.activation(
                    atile[:, h * T + qb4 * 128: h * T + qb4 * 128 + 128],
                    psO[par][:, 0:128], AF.Copy),
                    waits=[(mm, mm.n)], inc=sc)
                oev_s[(wt, it)] = sc.n

            for i in range(NIT + 4):
                if 1 <= i <= NIT:
                    p1a(i - 1)
                if 2 <= i <= NIT + 1 and not s_only:
                    p1b(i - 2)
                if 3 <= i <= NIT + 2 and not s_only and not no_p1c:
                    p1c(i - 3)
                if 4 <= i and not no_p2:
                    p2(i - 4)
                if i < NIT:
                    emit_S(i)
            if no_p2 or no_wo:
                if wt == 3:
                    S.emit("sync", None,
                           waits=[(vc, vc.n), (sc, sc.n), (mm, mm.n)])
                    with nc.Block("attn") as blk:
                        S.run_block(blk)
                    return
                continue
            a_done_sc = oev_s[(wt, NIT - 1)]

            for m in range(NCH):
                par = m % 2
                w = []
                if (wt, m - 2) in wo_fill:
                    w = [(mm, wo_fill[(wt, m - 2)])]
                elif wt > 0:
                    w = [(mm, wo_fill[(wt - 1, NCH - 2 + m)])]
                else:
                    w = [(mm, a1_mm)]
                S.emit("scalar", lambda e, m=m, par=par: e.dma_start(
                    wpan[par][:, :], P.wock[m]),
                    waits=w, inc=wld[par], inc_n=16)
                w2 = []
                if (wt, m - 2) in wo_stt:
                    w2 = [(vc, wo_stt[(wt, m - 2)])]
                elif wt > 0:
                    w2 = [(vc, wo_stt[(wt - 1, NCH - 2 + m)])]
                else:
                    w2 = [(mm, a1_mm), (vc, a1_vc), (sc, a1_sc)]
                S.emit("sync", lambda e, m=m, par=par, t0own=t0own: e.dma_start(
                    xm[par][:, :],
                    P.xcur[m * 128:(m + 1) * 128,
                           HALO + t0own: HALO + t0own + T]),
                    waits=w2, inc=xml[par], inc_n=16)

                w = [(wld[par], wld[par].n), (sc, a_done_sc)]
                if (wt, m - 2) in wo_stt:
                    w.append((vc, wo_stt[(wt, m - 2)]))
                elif wt > 0 and m < 2:
                    w.append((vc, wo_stt[(wt - 1, NCH - 2 + m)]))

                def wof(e, par=par):
                    ins = None
                    for kc in range(NCH):
                        ins = e.matmul(psO[par][:, :],
                                       wpan[par][:, kc * 128:(kc + 1) * 128],
                                       atile[:, kc * T:(kc + 1) * T],
                                       start=(kc == 0), stop=(kc == NCH - 1))
                    return ins
                S.emit("tensor", wof, waits=w, inc=mm)
                wo_fill[(wt, m)] = mm.n
                w = [(mm, mm.n), (xml[par], xml[par].n), (stb[par], stb[par].n)]
                S.emit("vector", lambda e, par=par: e.tensor_tensor(
                    ob[par][:, :], psO[par][:, :], xm[par][:, :], ALU.add),
                    waits=w, inc=vc)
                wo_stt[(wt, m)] = vc.n
                S.emit("sync", lambda e, m=m, par=par, t0own=t0own: e.dma_start(
                    P.xcur[m * 128:(m + 1) * 128,
                           HALO + t0own: HALO + t0own + T],
                    ob[par][:, :]),
                    waits=[(vc, vc.n)], inc=stb[par], inc_n=16)

        S.emit("sync", None, waits=[(stb[0], stb[0].n), (stb[1], stb[1].n)])
        with nc.Block("attn") as blk:
            S.run_block(blk)


def _host_inputs(L0_attn_x, L0_attn_Wq, L0_attn_Wk, L0_attn_Wv, L0_attn_Wo,
                 L0_ffn_SU_ug, L0_ffn_SU_dp, L0_ffn_Wscale_ug, L0_ffn_Wscale_dp,
                 L0_ffn_W_up, L0_ffn_W_gate, L0_ffn_W_down):
    bf = ml_dtypes.bfloat16
    f32 = np.float32

    def blockw(W, npan):
        W = np.asarray(W, f32)
        o, i = W.shape
        return np.ascontiguousarray(
            W.reshape(npan, 128, i // 128, 128).transpose(0, 3, 2, 1))

    shared = {
        "wupB": blockw(L0_ffn_W_up, ICH),
        "wgB": blockw(L0_ffn_W_gate, ICH),
        "wdB": blockw(L0_ffn_W_down, NCH),
        "wqB": blockw(L0_attn_Wq, NH),
        "wkB": blockw(L0_attn_Wk, NKV),
        "wvB": blockw(L0_attn_Wv, NKV),
        "woB": blockw(L0_attn_Wo, NCH),
        "su_pc": np.ascontiguousarray(
            np.asarray(L0_ffn_SU_ug, f32).reshape(NCH, 128).T),
        "wsu_pc": np.ascontiguousarray(
            np.asarray(L0_ffn_Wscale_ug, f32)[:INTER].reshape(ICH, 128).T),
        "wsg_pc": np.ascontiguousarray(
            np.asarray(L0_ffn_Wscale_ug, f32)[INTER:].reshape(ICH, 128).T),
        "sudp_pc": np.ascontiguousarray(
            np.asarray(L0_ffn_SU_dp, f32).reshape(ICH, 128).T),
        "wsdp_pc": np.ascontiguousarray(
            np.asarray(L0_ffn_Wscale_dp, f32).reshape(NCH, 128).T),
    }
    ij = np.arange(128)
    pop = np.zeros((128, 128), np.int64)
    for b in range(7):
        pop += ((ij[:, None] >> b) & 1) * ((ij[None, :] >> b) & 1)
    shared["h128"] = np.where(pop % 2 == 0, 1.0, -1.0).astype(bf)
    shared["iden"] = np.eye(128).astype(bf)
    shared["onesk"] = np.ones((128, 1), bf)
    shared["onesr"] = np.ones((1, 128), f32)
    shared["onesb"] = np.ones((1, 128), bf)
    p = np.arange(128)[:, None]
    x = np.arange(640)[None, :]
    valid = np.where(x < 512, x > p, (x - 512) <= p)
    shared["bmsk"] = np.where(valid, 0.0, NEG).astype(bf)
    shared["fconst"] = np.array([[EPS, 0.0]], f32)

    x4 = np.asarray(L0_attn_x, f32)
    scale = 1.0 / math.sqrt(DH)
    inv_freq = (10000.0 ** (-np.arange(0, DH, 2, dtype=np.float64) / DH))

    in_maps = []
    for core in range(8):
        b, half = divmod(core, 2)
        if half == 0:
            xloc = np.concatenate(
                [np.zeros((HALO, DM), f32), x4[b, 0:TOWN]], axis=0)
            abspos = np.arange(-HALO, TOWN)
            krow = np.concatenate(
                [np.full(HALO, NEG, f32), np.zeros(TOWN, f32)])
        else:
            xloc = x4[b, TOWN - HALO: 4096]
            abspos = np.arange(TOWN - HALO, 4096)
            krow = np.zeros(TLOC, f32)
        pos = np.maximum(abspos, 0)
        ang = pos[:, None].astype(np.float64) * inv_freq[None, :]
        c64 = np.cos(ang).astype(f32)
        s64 = np.sin(ang).astype(f32)
        cosf = np.concatenate([c64, c64], axis=1).T
        sinf = np.concatenate([-s64, s64], axis=1).T
        m = dict(shared)
        m["xin"] = np.ascontiguousarray(xloc.T)
        m["cosq"] = np.ascontiguousarray(cosf * scale).astype(f32)
        m["sinq"] = np.ascontiguousarray(sinf * scale).astype(f32)
        m["cosk"] = np.ascontiguousarray(cosf)
        m["sink"] = np.ascontiguousarray(sinf)
        m["krow"] = krow[None, :].astype(bf)
        in_maps.append(m)
    return in_maps


def kernel(**inputs):
    in_maps = _host_inputs(**inputs)
    nc = build_program()
    res = run_bass_kernel_spmd(nc, in_maps, core_ids=list(range(8)))
    out = np.empty((4, 4096, DM), np.float32)
    for core in range(8):
        b, half = divmod(core, 2)
        out[b, half * TOWN:(half + 1) * TOWN, :] = res.results[core]["xout"].T
    return out
